# version 13
# baseline (speedup 1.0000x reference)
import math
import numpy as np
import concourse.bass as bass
import concourse.mybir as mybir
from concourse.bass import RegisterHandles
from concourse.expressions_rust import RuntimeValue
from concourse.bass_utils import run_bass_kernel_spmd

F32 = mybir.dt.float32
BF16 = mybir.dt.bfloat16
I32 = mybir.dt.int32
U8 = mybir.dt.uint8
AF = mybir.ActivationFunctionType
ALU = mybir.AluOpType
AX = mybir.AxisListType
PE = mybir.EngineType.PE
POOL = mybir.EngineType.Pool

S = 8192
D = 4096
NH = 32
QL = 1024
NDT = 32
SCALE = 1.0 / math.sqrt(128.0)
EPS = 1e-5
NE, BN, RD = 64, 512, 16
INV_D = 1.0 / 4096.0
NIT = 68
NSB = 18

_CACHE = {}
_DEBUG = False


def build_nc():
    nc = bass.Bass("TRN2", target_bir_lowering=False)
    cnt = {}
    sems = {}
    for name in ("dC", "dCg", "vC", "dX0", "dX1", "tTP", "vXT", "tKV", "sKV", "tTV", "vTV",
                 "dKV0", "dKV1", "dWQ0", "dWQ1", "tQ", "sQ", "dST", "semS", "semP", "semT",
                 "tST", "sLE", "dRC0", "dRC1", "dRB0", "dRB1", "vAE", "vAEr", "vAC", "tAV", "dWO0", "dWO1", "dXR0", "dXR1", "tW", "vW",
                 "vLN", "sLN", "vDC", "tT2", "vT2", "tR", "sR", "tSc", "vSc",
                 "vTK", "sTK", "tGT", "sGT", "tGX", "sGX", "dAT", "tZ", "vZ",
                 "dBT", "tB", "vB", "sOUT", "dY0", "dY1", "dY2", "dY3", "dE8", "dDBG", "vQC", "dWC", "vPS"):
        sems[name] = nc.alloc_semaphore(name)
        cnt[name] = 0

    def sem(name):
        return sems[name]

    def inc(ins, name, n):
        ins.then_inc(sems[name], n)
        cnt[name] += n
        return ins

    din = {}
    for nm, shp in (("xfull", [S, D]), ("xown", [QL, D]), ("wqT", [D, D]),
                    ("wkT", [D, 128]), ("wvT", [D, 128]), ("woT", [D, D]),
                    ("aT", [D, BN]), ("bT", [BN, D]), ("wqrT", [D, RD]),
                    ("keysT", [RD, NE]), ("wqb", [128, NH]), ("wkb", [128, 1]),
                    ("wvb", [128, 1]), ("wob", [1, D]), ("ident", [128, 128]),
                    ("tri", [128, 896]), ("e8", [64, 512])):
        din[nm] = nc.dram_tensor(nm, shp, F32, kind="ExternalInput").ap()
    d_tbl = nc.dram_tensor("tbl", [1, 32], I32, kind="ExternalInput").ap()
    d_y = nc.dram_tensor("y", [QL, D], F32, kind="ExternalOutput").ap()
    d_kT = nc.dram_tensor("kT_scr", [128, S], BF16, kind="Internal").ap()
    d_v = nc.dram_tensor("v_scr", [128, S], BF16, kind="Internal").ap()
    d_rcp = nc.dram_tensor("rcp_scr", [2, 1024], F32, kind="Internal").ap()
    d_wob = nc.dram_tensor("wo_scr", [D, D], BF16, kind="Internal").ap()
    d_wobb = nc.dram_tensor("wob_scr", [1, D], BF16, kind="Internal").ap()
    if _DEBUG:
        d_dbgq = nc.dram_tensor("dbgq", [128, 32768], BF16, kind="ExternalOutput").ap()
        d_dbga = nc.dram_tensor("dbga", [128, 32768], BF16, kind="ExternalOutput").ap()

    A = nc.alloc_sbuf_tensor
    kT8 = A("kT8", [128, 4096], BF16)
    vS8 = A("vS8", [128, 4096], BF16)
    arena0 = A("arena0", [128, 65536], U8)
    arena1 = A("arena1", [128, 65536], U8)
    slab = A("slab", [128, 49152], U8)
    idb = A("idb", [128, 128], BF16)
    onesb = A("onesb", [128, 1], BF16)
    onesf = A("onesf", [128, 1], F32)
    ones1b = A("ones1b", [1, 128], BF16)
    ones1f = A("ones1f", [1, 128], F32)
    wqb_sb = A("wqb_sb", [128, NH], F32)
    wkb_sb = A("wkb_sb", [128, 1], F32)
    wvb_sb = A("wvb_sb", [128, 1], F32)
    wobt = A("wobt", [1, 512], BF16)
    keysb = A("keysb", [RD, NE], BF16)
    wqr_sb = A("wqr_sb", [128, NDT * RD], BF16)
    tbl_sb = A("tbl_sb", [1, 32], I32)
    s1p = A("s1p", [128, 64], F32)
    st_a = A("st_a", [128, 1], F32)
    st_b = A("st_b", [128, 1], F32)
    st_c = A("st_c", [128, 1], F32)
    st_d = A("st_d", [128, 1], F32)
    st4a = A("st4a", [128, 4], F32)
    st4b = A("st4b", [128, 4], F32)
    st4c = A("st4c", [128, 4], F32)
    mx_sb = A("mx_sb", [128, 16], F32)
    gate_sb = A("gate_sb", [128, 64], BF16)
    gT_sb = A("gT_sb", [64, 128], BF16)
    qr_sbT = A("qr_sbT", [RD, 512], BF16)
    qcur = A("qcur", [128, 1024], BF16)
    p_sb4 = A("p_sb4", [128, 4096], BF16)

    a0b = arena0.ap().bitcast(BF16)
    a0f = arena0.ap().bitcast(F32)
    a1b = arena1.ap().bitcast(BF16)
    a1f = arena1.ap().bitcast(F32)
    slb = slab.ap().bitcast(BF16)
    slf = slab.ap().bitcast(F32)

    wkTb = a1b[:, 16384:20480]
    wvTb = a1b[:, 20480:24576]
    ktmp = slb[:, 16384:17408]
    vtt = slb[:, 17408:18432]
    vtmp = slb[:, 18432:19456]
    kstage = slb[:, 16384:19968]
    vstage = slb[:, 19968:23552]
    kdiag = slb[:, 23552:24576]
    trib = slb[:, 0:896]
    zerb = slb[:, 896:1408]
    vdiag = slb[:, 2432:3456]
    rcp_sb = [slf[0:1, 1728:2752], slf[0:1, 2752:3776]]
    rb_sb = [slf[:, 3776:4800], slf[:, 4800:5824]]
    acc_sb = slf[:, 5824:6848]
    acc0_sb = slf[:, 6848:7872]
    ptmp = slb[:, 15744:16256]
    xres = slf[:, 8192:10240]
    gx = kT8.ap()[:, 0:2048]
    e8b = kT8.ap()[0:64, 2048:2560]
    kf = kT8.ap().bitcast(F32)
    sc_sb = kf[:, 1280:1536]
    fl_sb = kf[:, 1536:1600]
    zz_sb = kf[:, 1600:1664]
    msk_sb = kf[:, 1664:1728]
    ex_sb = kf[:, 1728:1792]
    ge_sb = kf[:, 1792:1856]

    pSA = nc.alloc_psum_tensor("pSA", [128, 2048], F32).ap()
    pA0 = nc.alloc_psum_tensor("pA0", [128, 1024], F32).ap()
    pSum2 = nc.alloc_psum_tensor("pSum2", [1, 1024], F32).ap()
    pbA = pA0
    pbB = pSA[:, 1024:2048]
    pbSum = pSum2
    pS = [pSA[:, 0:512], pSA[:, 512:1024]]
    pSpair = [pSA[:, 0:1024], pSA[:, 1024:2048]]
    pSh = [[pSA[:, 0:512], pSA[:, 512:1024]],
           [pSA[:, 1024:1536], pSA[:, 1536:2048]]]
    pSAb = pSA.bitcast(BF16)
    pSb = [pSAb[:, 0:1024], pSAb[:, 1024:2048]]
    pbBb = pSAb[:, 2048:4096]
    pbAb = pA0.bitcast(BF16)

    with nc.Block() as blk:
        @blk.sync
        def _(sy):
            for dst, src in ((wqb_sb, din["wqb"]), (wkb_sb, din["wkb"]),
                             (wvb_sb, din["wvb"])):
                inc(sy.dma_start(out=dst[:], in_=src), "dC", 16)
            inc(sy.dma_start(out=tbl_sb[:], in_=d_tbl), "dC", 16)

        @blk.gpsimd
        def _(g):
            inc(g.dma_start(out=idb[:], in_=din["ident"]), "dCg", 16)
            inc(g.dma_start(out=keysb[:], in_=din["keysT"]), "dCg", 16)
            inc(g.dma_start(
                out=wqr_sb[:].rearrange("p (t f) -> p t f", t=NDT),
                in_=din["wqrT"].rearrange("(t p) f -> p t f", p=128)), "dCg", 16)
            inc(g.dma_start(
                out=wkTb[:].rearrange("p (t f) -> p t f", t=NDT),
                in_=din["wkT"].rearrange("(t p) f -> p t f", p=128)), "dCg", 16)
            inc(g.dma_start(
                out=wvTb[:].rearrange("p (t f) -> p t f", t=NDT),
                in_=din["wvT"].rearrange("(t p) f -> p t f", p=128)), "dCg", 16)

        @blk.vector
        def _(v):
            v.memset(onesb[:], 1.0)
            v.memset(onesf[:], 1.0)
            v.memset(ones1b[:], 1.0)
            inc(v.memset(ones1f[:], 1.0), "vC", 1)

    def xbuf(sb):
        return slb if (sb % 2 == 0 or sb >= 16) else a0b

    with nc.Block() as blk:
        @blk.gpsimd
        def _(g):
            for sb in range(NSB):
                if sb >= 2:
                    g.wait_ge(sem("tTP"), ((sb - 1) if sb < 16 else sb) * 32)
                src = din["xfull"] if sb < 16 else din["xown"]
                r0 = 512 * sb if sb < 16 else 512 * (sb - 16)
                for rt in range(4):
                    inc(g.dma_start(
                        out=xbuf(sb)[:, 4096 * rt:4096 * (rt + 1)],
                        in_=src[r0 + 128 * rt:r0 + 128 * (rt + 1), :]),
                        "dX%d" % (sb % 2), 16)

        @blk.tensor
        def _(t):
            t.wait_ge(sem("dC"), cnt["dC"])
            t.wait_ge(sem("dCg"), cnt["dCg"])
            t.wait_ge(sem("vC"), cnt["vC"])
            for sb in range(NSB):
                t.wait_ge(sem("dX%d" % (sb % 2)), (sb // 2 + 1) * 64)
                for dt in range(NDT):
                    gidx = sb * 32 + dt
                    par = gidx % 2
                    if gidx >= 2:
                        t.wait_ge(sem("vXT"), gidx - 1)
                    for rt in range(4):
                        ins = t.transpose(
                            pSb[par][:, 128 * rt:128 * (rt + 1)],
                            xbuf(sb)[:, 4096 * rt + 128 * dt:4096 * rt + 128 * (dt + 1)],
                            idb[:])
                    inc(ins, "tTP", 1)
                if sb < 16:
                    t.wait_ge(sem("vXT"), (sb + 1) * 32)
                    if sb > 0:
                        t.wait_ge(sem("sKV"), sb * 2)
                    for dt in range(NDT):
                        t.matmul(pbA[:, 0:512], wkTb[:, 128 * dt:128 * (dt + 1)],
                                 a1b[:, 512 * dt:512 * (dt + 1)],
                                 start=(dt == 0), stop=(dt == NDT - 1))
                    for dt in range(NDT):
                        ins = t.matmul(pbA[:, 512:1024], wvTb[:, 128 * dt:128 * (dt + 1)],
                                       a1b[:, 512 * dt:512 * (dt + 1)],
                                       start=(dt == 0), stop=(dt == NDT - 1))
                    inc(ins, "tKV", 1)
                    t.wait_ge(sem("sKV"), sb * 2 + 2)
                    if sb > 0:
                        t.wait_ge(sem("vTV"), sb)
                    for j in range(4):
                        ins = t.transpose(
                            pbBb[:, 128 * j:128 * (j + 1)],
                            vtt[:, 512 * (sb % 2) + 128 * j:512 * (sb % 2) + 128 * (j + 1)],
                            idb[:])
                    inc(ins, "tTV", 1)

        @blk.scalar
        def _(s_):
            for sb in range(16):
                s_.wait_ge(sem("tKV"), sb + 1)
                if sb >= 2:
                    s_.wait_ge(sem("dKV%d" % (sb % 2)), (sb // 2) * 32)
                kdst = kT8[:, 512 * sb:512 * (sb + 1)] if sb < 8 \
                    else ktmp[:, 512 * (sb % 2):512 * (sb % 2) + 512]
                inc(s_.activation(kdst, pbA[:, 0:512], AF.Identity,
                                  bias=wkb_sb[:], scale=1.0), "sKV", 1)
                inc(s_.activation(vtt[:, 512 * (sb % 2):512 * (sb % 2) + 512],
                                  pbA[:, 512:1024], AF.Identity,
                                  bias=wvb_sb[:], scale=1.0), "sKV", 1)

        @blk.vector
        def _(v):
            for sb in range(NSB):
                for dt in range(NDT):
                    gidx = sb * 32 + dt
                    par = gidx % 2
                    v.wait_ge(sem("tTP"), gidx + 1)
                    if 0 < sb < 16 and dt == 0:
                        v.wait_ge(sem("tKV"), sb)
                    if sb < 16:
                        dst = a1b[:, 512 * dt:512 * (dt + 1)]
                    else:
                        c0 = 1024 * dt + 512 * (sb - 16)
                        dst = a0b[:, c0:c0 + 512]
                    inc(v.tensor_copy(dst, pSb[par][:, 0:512]), "vXT", 1)
                if sb < 16:
                    v.wait_ge(sem("tTV"), sb + 1)
                    vdst2 = vS8[:, 512 * sb:512 * (sb + 1)] if sb < 8 \
                        else vtmp[:, 512 * (sb % 2):512 * (sb % 2) + 512]
                    inc(v.tensor_copy(vdst2, pbBb[:, 0:512]), "vTV", 1)

        @blk.sync
        def _(sy):
            for sb in range(16):
                sy.wait_ge(sem("sKV"), sb * 2 + 1)
                ksrc = kT8[:, 512 * sb:512 * (sb + 1)] if sb < 8 \
                    else ktmp[:, 512 * (sb % 2):512 * (sb % 2) + 512]
                inc(sy.dma_start(out=d_kT[:, 512 * sb:512 * (sb + 1)], in_=ksrc),
                    "dKV%d" % (sb % 2), 16)
                sy.wait_ge(sem("vTV"), sb + 1)
                vsrc = vS8[:, 512 * sb:512 * (sb + 1)] if sb < 8 \
                    else vtmp[:, 512 * (sb % 2):512 * (sb % 2) + 512]
                inc(sy.dma_start(out=d_v[:, 512 * sb:512 * (sb + 1)], in_=vsrc),
                    "dKV%d" % (sb % 2), 16)

    with nc.Block() as blk:
        @blk.gpsimd
        def _(g):
            for hp in range(16):
                if hp >= 2:
                    g.wait_ge(sem("tQ"), (hp - 1) * 4)
                else:
                    g.wait_ge(sem("tTP"), NSB * 32)
                inc(g.dma_start(
                    out=slb[:, 8192 * (hp % 2):8192 * (hp % 2) + 8192]
                        .rearrange("p (t f) -> p t f", t=NDT),
                    in_=din["wqT"][:, 256 * hp:256 * (hp + 1)]
                        .rearrange("(t p) f -> p t f", p=128)),
                    "dWQ%d" % (hp % 2), 16)

        @blk.tensor
        def _(t):
            t.wait_ge(sem("vXT"), NSB * 32)
            for hp in range(16):
                t.wait_ge(sem("dWQ%d" % (hp % 2)), (hp // 2 + 1) * 16)
                for hh in range(2):
                    h = 2 * hp + hh
                    for qh in range(2):
                        m = hp * 4 + hh * 2 + qh
                        if m >= 2:
                            t.wait_ge(sem("sQ"), m - 1)
                        for dt in range(NDT):
                            ins = t.matmul(
                                pbA[:, 512 * qh:512 * (qh + 1)],
                                slb[:, 8192 * (hp % 2) + 256 * dt + 128 * hh:
                                    8192 * (hp % 2) + 256 * dt + 128 * (hh + 1)],
                                a0b[:, 1024 * dt + 512 * qh:1024 * dt + 512 * (qh + 1)],
                                start=(dt == 0), stop=(dt == NDT - 1))
                        inc(ins, "tQ", 1)

        @blk.scalar
        def _(s_):
            for hp in range(16):
                for hh in range(2):
                    h = 2 * hp + hh
                    for qh in range(2):
                        m = hp * 4 + hh * 2 + qh
                        s_.wait_ge(sem("tQ"), m + 1)
                        inc(s_.activation(
                            a1b[:, 1024 * h + 512 * qh:1024 * h + 512 * (qh + 1)],
                            pbA[:, 512 * qh:512 * (qh + 1)], AF.Identity,
                            bias=wqb_sb[:, h:h + 1], scale=1.0), "sQ", 1)

    with nc.Block() as blk:
        @blk.gpsimd
        def _(g):
            g.wait_ge(sem("dKV0"), 256)
            g.wait_ge(sem("dKV1"), 256)
            g.wait_ge(sem("tQ"), NH * 2)
            inc(g.dma_start(out=trib, in_=din["tri"]), "dST", 16)
            _, vals = nc.values_load_multi_w_load_instructions(
                tbl_sb[0:1, 0:16], engines=[POOL],
                min_val=0, max_val=S - 512,
                skip_runtime_bounds_check=True)
            for gi in range(7):
                inc(g.dma_start(out=kstage[:, 512 * gi:512 * (gi + 1)],
                                in_=d_kT[:, bass.ds(vals[gi], 512)]), "dST", 16)
                inc(g.dma_start(out=vstage[:, 512 * gi:512 * (gi + 1)],
                                in_=d_v[:, bass.ds(vals[gi], 512)]), "dST", 16)
            for si in range(2):
                inc(g.dma_start(out=kdiag[:, 512 * si:512 * (si + 1)],
                                in_=d_kT[:, bass.ds(vals[14 + si], 512)]), "dST", 16)
                inc(g.dma_start(out=vdiag[:, 512 * si:512 * (si + 1)],
                                in_=d_v[:, bass.ds(vals[14 + si], 512)]), "dST", 16)

        @blk.vector
        def _(v):
            v.wait_ge(sem("tQ"), NH * 2)
            inc(v.memset(zerb, 0.0), "vC", 1)

    with nc.Block() as blk:
        @blk.gpsimd
        def _(g):
            for og in range(16):
                g.wait_ge(sem("semT"), (NIT // 2) * ((3 * og) // 2))
                inc(g.dma_start(out=d_wob[:, 256 * og:256 * (og + 1)],
                                in_=din["woT"][:, 256 * og:256 * (og + 1)]),
                    "dWC", 16)
            inc(g.dma_start(out=d_wobb[:], in_=din["wob"]), "dWC", 16)

        if _DEBUG:
            @blk.sync
            def _(sy):
                sy.wait_ge(sem("sQ"), NH * 2)
                inc(sy.dma_start(out=d_dbgq, in_=a1b[:, 0:32768]), "dDBG", 16)

        @blk.tensor
        def _(t):
            t.wait_ge(sem("dST"), cnt["dST"])
            t.wait_ge(sem("vC"), cnt["vC"])
            t.wait_ge(sem("sQ"), cnt["sQ"])
            qreg = [nc.alloc_register(PE, f"qoff{g}") for g in range(7)]
            for g in range(7):
                t.reg_load(qreg[g], tbl_sb[0:1, 7 + g:8 + g])
            qw = [RuntimeValue(RegisterHandles((qreg[g],)), 0, 512)
                  for g in range(7)]
            NP = NIT // 2

            def emit_fold(hh):
                t.wait_ge(sem("vAC"), 20 * (hh + 1))
                t.matmul(pSum2[0:1, 512:1024], onesf[:], acc_sb[:, 0:512],
                         start=False, stop=False)
                t.matmul(pSum2[0:1, 0:512], onesf[:], acc0_sb[:, 0:512],
                         start=False, stop=False)
                t.matmul(pSum2[0:1, 0:512], onesf[:], acc0_sb[:, 512:1024],
                         start=False, stop=True)
                inc(t.matmul(pSum2[0:1, 512:1024], onesf[:],
                             acc_sb[:, 512:1024], start=False, stop=True),
                    "tST", 1)

            for h in range(NH):
                bP = h * NP
                pA = pA0
                t.wait_ge(sem("vQC"), h + 1)
                plan = []
                for i in range(NIT):
                    par = i % 2
                    if i < 32:
                        lhsK = kT8[:, 128 * i:128 * (i + 1)]
                        lhsV = vS8[:, 128 * i:128 * (i + 1)]
                        qrhs = qcur[:, 512:1024]
                        aout = pA[:, 512:1024]
                        sout = pSum2[0:1, 512:1024]
                        diag = False
                    elif i < 60:
                        gi, j = (i - 32) // 4, (i - 32) % 4
                        lhsK = kstage[:, 512 * gi + 128 * j:512 * gi + 128 * (j + 1)]
                        lhsV = vstage[:, 512 * gi + 128 * j:512 * gi + 128 * (j + 1)]
                        qrhs = qcur[:, bass.ds(qw[gi], 512)]
                        aout = pA[:, bass.ds(qw[gi], 512)]
                        sout = pSum2[0:1, bass.ds(qw[gi], 512)]
                        diag = False
                    else:
                        si, dl = (i - 60) // 4, (i - 60) % 4
                        lhsK = kdiag[:, 512 * si + 128 * dl:512 * si + 128 * (dl + 1)]
                        lhsV = vdiag[:, 512 * si + 128 * dl:512 * si + 128 * (dl + 1)]
                        qrhs = qcur[:, 512 * si:512 * (si + 1)]
                        aout = pA[:, 512 * si:512 * (si + 1)]
                        sout = pSum2[0:1, 512 * si:512 * (si + 1)]
                        diag = True
                    plan.append((par, diag, dl if diag else 0, lhsK, lhsV,
                                 qrhs, aout, sout))

                def emit_spair(p):
                    sl = pSh[p % 2]
                    for j in (2 * p, 2 * p + 1):
                        parj, diagj, dlj, lhsKj, _, qrhsj, _, _ = plan[j]
                        if diagj:
                            t.matmul(sl[parj], idb[:],
                                     trib[:, 384 - 128 * dlj:896 - 128 * dlj],
                                     start=True, stop=False)
                            ins = t.matmul(sl[parj], lhsKj, qrhsj, start=False,
                                           stop=True)
                        else:
                            ins = t.matmul(sl[parj], lhsKj, qrhsj, start=True,
                                           stop=True)
                    inc(ins, "semS", 1)

                if h > 0:
                    t.wait_ge(sem("semP"), bP - 1)
                emit_spair(0)
                if h > 0:
                    t.wait_ge(sem("semP"), bP)
                emit_spair(1)
                if h > 0:
                    emit_fold(h - 1)
                t.wait_ge(sem("vAEr"), h)
                t.matmul(pA[:, 0:512], idb[:], zerb, start=True, stop=False)
                t.matmul(pA[:, 512:1024], idb[:], zerb, start=True, stop=False)
                for p in range(NP):
                    t.wait_ge(sem("semP"), bP + p + 1)
                    if p == 16:
                        t.wait_ge(sem("sLE"), 2 * h)
                        t.matmul(pSum2[0:1, 0:512], onesb[:], zerb,
                                 start=True, stop=False)
                        t.matmul(pSum2[0:1, 512:1024], onesb[:], zerb,
                                 start=True, stop=False)
                    for i in (2 * p, 2 * p + 1):
                        par, _, _, _, lhsV, _, aout, sout = plan[i]
                        pcol = 512 * (i % 8)
                        ins = t.matmul(aout, lhsV, p_sb4[:, pcol:pcol + 512],
                                       start=False, stop=False)
                    if 16 <= p < 30:
                        t.wait_ge(sem("vPS"), h * 14 + (p - 16) + 1)
                        ins = t.matmul(plan[2 * p][7], onesb[:], ptmp[:],
                                       start=False, stop=False)
                    inc(ins, "semT", 1)
                    if p + 2 < NP:
                        emit_spair(p + 2)
                t.matmul(pA[:, 0:512], idb[:], zerb, start=False, stop=True)
                inc(t.matmul(pA[:, 512:1024], idb[:], zerb, start=False,
                             stop=True), "tAV", 1)
            emit_fold(NH - 1)

        @blk.scalar
        def _(s_):
            NP = NIT // 2

            def emit_lnexp(hh):
                rp = hh % 2
                s_.wait_ge(sem("tST"), hh + 1)
                if hh >= 2:
                    s_.wait_ge(sem("dRC0" if hh % 2 == 0 else "dRC1"),
                               16 * (hh // 2))
                inc(s_.activation(rcp_sb[rp], pSum2[0:1, :], AF.Ln,
                                  bias=0.0, scale=1.0), "sLE", 1)
                s_.wait_ge(sem("sLE"), 2 * hh + 1)
                inc(s_.activation(rcp_sb[rp], rcp_sb[rp], AF.Exp,
                                  bias=0.0, scale=-1.0), "sLE", 1)

            for h in range(NH):
                bS = h * NP
                bT = h * NP
                for p in range(NP):
                    if p == 2 and h > 0:
                        emit_lnexp(h - 1)
                    s_.wait_ge(sem("semS"), bS + p + 1)
                    if bT + p - 3 > 0:
                        s_.wait_ge(sem("semT"), bT + p - 3)
                    if p < 4:
                        if h > 0:
                            s_.wait_ge(sem("vAC"), 20 * (h - 1) + 17 + p)
                    elif p <= 19:
                        s_.wait_ge(sem("vAC"), 20 * h + p - 3)
                    else:
                        s_.wait_ge(sem("vPS"), h * 14 + p - 19)
                    pcol = 1024 * (p % 4)
                    inc(s_.activation(p_sb4[:, pcol:pcol + 1024],
                                      pSpair[p % 2], AF.Exp,
                                      bias=0.0, scale=SCALE), "semP", 1)
            emit_lnexp(NH - 1)

        @blk.sync
        def _(sy):
            for h in range(NH):
                rp = h % 2
                j = h // 2
                sy.wait_ge(sem("sLE"), 2 * h + 2)
                inc(sy.dma_start(out=d_rcp[rp:rp + 1, :], in_=rcp_sb[rp]),
                    "dRC0" if rp == 0 else "dRC1", 16)
                sy.wait_ge(sem("dRC0" if rp == 0 else "dRC1"), 16 * (j + 1))
                if h >= 2:
                    sy.wait_ge(sem("vAE"), 2 * (h - 1))
                inc(sy.dma_start(out=rb_sb[rp],
                                 in_=d_rcp[rp:rp + 1, :].partition_broadcast(128)),
                    "dRB0" if rp == 0 else "dRB1", 16)

        @blk.vector
        def _(v):
            NP = NIT // 2

            def emit_norm(hh):
                rp = hh % 2
                v.wait_ge(sem("vAEr"), hh + 1)
                v.wait_ge(sem("dRB0" if hh % 2 == 0 else "dRB1"),
                          16 * (hh // 2 + 1))
                for si in range(2):
                    sl = a0b[:, 16384 * si + 512 * hh:
                             16384 * si + 512 * (hh + 1)]
                    inc(v.tensor_tensor(sl, sl,
                                        rb_sb[rp][:, 512 * si:512 * (si + 1)],
                                        ALU.mult), "vAE", 1)

            for h in range(NH):
                if h == 0:
                    v.wait_ge(sem("sQ"), NH * 2)
                else:
                    v.wait_ge(sem("semS"), NP * h)
                inc(v.tensor_copy(qcur[:], a1b[:, 1024 * h:1024 * (h + 1)]),
                    "vQC", 1)
                for p in range(NP):
                    if 16 <= p < 30:
                        v.wait_ge(sem("semP"), h * NP + p + 1)
                        v.wait_ge(sem("semT"), h * NP + p)
                        pcol = 1024 * (p % 4)
                        inc(v.tensor_tensor(
                            ptmp[:],
                            p_sb4[:, pcol:pcol + 512],
                            p_sb4[:, pcol + 512:pcol + 1024], ALU.add),
                            "vPS", 1)
                        continue
                    v.wait_ge(sem("semP"), h * NP + p + 1)
                    pcol = 1024 * (p % 4)
                    dst = acc0_sb if p in (30, 31) else acc_sb
                    if p in (0, 30):
                        if p == 0 and h > 0:
                            v.wait_ge(sem("tST"), h)
                        inc(v.tensor_copy(dst, p_sb4[:, pcol:pcol + 1024]),
                            "vAC", 1)
                    else:
                        v.wait_ge(sem("vAC"), cnt["vAC"])
                        inc(v.tensor_tensor(dst, dst,
                                            p_sb4[:, pcol:pcol + 1024], ALU.add),
                            "vAC", 1)
                    if p == 10 and h >= 2:
                        emit_norm(h - 2)
                v.wait_ge(sem("tAV"), h + 1)
                for si in range(2):
                    ins = v.tensor_copy(
                        a0b[:, 16384 * si + 512 * h:16384 * si + 512 * (h + 1)],
                        pA0[:, 512 * si:512 * (si + 1)])
                inc(ins, "vAEr", 1)
            emit_norm(NH - 2)
            emit_norm(NH - 1)

    dcbuf = [vS8[:, 0:4096], p_sb4[:, 0:4096]]
    esnap = {}

    def emit_D(qsb, prev_ln2):
        tZ_prev = cnt["tZ"]
        tB_prev = cnt["tB"]
        vW_prev = cnt["vW"]
        vB_prev = cnt["vB"]
        vZ_prev = cnt["vZ"]
        if prev_ln2:
            lb = {"vB2": cnt["vB"] - 32, "sq2": cnt["sLN"],
                  "sOUT": cnt["sOUT"], "pq": qsb - 1}
        with nc.Block() as blk:
            @blk.gpsimd
            def _(g):
                if qsb == 0:
                    g.wait_ge(sem("vAE"), 2 * NH)
                    g.wait_ge(sem("dRB0"), 16 * (NH // 2))
                    g.wait_ge(sem("dRB1"), 16 * (NH // 2))
                    g.wait_ge(sem("dWC"), 17 * 16)
                    inc(g.dma_start(out=e8b, in_=din["e8"]), "dE8", 16)
                for og in range(16):
                    idx = 16 * qsb + og
                    if idx == 16:
                        g.wait_ge(sem("tZ"), tZ_prev)
                    if idx == 17:
                        g.wait_ge(sem("tB"), tB_prev)
                    if idx >= 2:
                        g.wait_ge(sem("tW"), (idx - 1) * 4)
                    inc(g.dma_start(
                        out=slb[:, 8192 * (idx % 2):8192 * (idx % 2) + 8192]
                            .rearrange("p (t f) -> p t f", t=NDT),
                        in_=d_wob[:, 256 * og:256 * (og + 1)]
                            .rearrange("(t p) f -> p t f", p=128)),
                        "dWO%d" % (idx % 2), 16)
                    inc(g.dma_start(
                        out=wobt[0:1, 256 * (idx % 2):256 * (idx % 2) + 256],
                        in_=d_wobb[0:1, 256 * og:256 * (og + 1)]),
                        "dWO%d" % (idx % 2), 16)

            @blk.sync
            def _(sy):
                def xres_og(og):
                    idx = 16 * qsb + og
                    if idx >= 2:
                        sy.wait_ge(sem("vW"), (idx - 1) * 4)
                    elif idx == 0:
                        sy.wait_ge(sem("vAE"), 2 * NH)
                        sy.wait_ge(sem("dRB0"), 16 * (NH // 2))
                        sy.wait_ge(sem("dRB1"), 16 * (NH // 2))
                    if idx == 16:
                        sy.wait_ge(sem("vW"), vW_prev)
                    for qt in range(4):
                        inc(sy.dma_start(
                            out=xres[:, 1024 * (idx % 2) + 256 * qt:
                                     1024 * (idx % 2) + 256 * (qt + 1)],
                            in_=din["xown"][512 * qsb + 128 * qt:
                                            512 * qsb + 128 * (qt + 1),
                                            256 * og:256 * (og + 1)]),
                            "dXR%d" % (idx % 2), 16)

                if prev_ln2:
                    xres_og(0)
                    xres_og(1)
                    ln2_store(sy, lb)
                    for og in range(2, 16):
                        xres_og(og)
                else:
                    for og in range(16):
                        xres_og(og)

            if _DEBUG and qsb == 0:
                @blk.sync
                def _(sy):
                    sy.wait_ge(sem("vAE"), 2 * NH)
                    inc(sy.dma_start(out=d_dbga, in_=a0b[:, 0:32768]), "dDBG", 16)

            @blk.vector
            def _(v):
                if prev_ln2:
                    ln2_vector(v, lb)
                for og in range(16):
                    idx = 16 * qsb + og
                    for qt in range(4):
                        par = qt % 2
                        gq = idx * 4 + qt
                        if qsb == 1 and og == 0:
                            v.wait_ge(sem("dY%d" % qt), 16)
                        v.wait_ge(sem("tW"), gq + 1)
                        inc(v.scalar_tensor_tensor(
                            a1f[:, 4096 * qt + 256 * og:4096 * qt + 256 * (og + 1)],
                            pS[par][:, 0:256], 1.0,
                            xres[:, 1024 * (idx % 2) + 256 * qt:
                                 1024 * (idx % 2) + 256 * (qt + 1)],
                            ALU.mult, ALU.add,
                            accum_out=s1p[:, 16 * qt + og:16 * qt + og + 1]),
                            "vW", 1)

            if prev_ln2:
                @blk.scalar
                def _(s_):
                    ln2_scalar(s_, lb)

            @blk.tensor
            def _(t):
                if qsb == 0:
                    t.wait_ge(sem("vAE"), cnt["vAE"])
                    t.wait_ge(sem("semP"), cnt["semP"])
                    if _DEBUG:
                        t.wait_ge(sem("dDBG"), 16)
                else:
                    t.wait_ge(sem("vZ"), vZ_prev)
                    t.wait_ge(sem("vB"), vB_prev)
                for og in range(16):
                    idx = 16 * qsb + og
                    t.wait_ge(sem("dWO%d" % (idx % 2)), (idx // 2 + 1) * 32)
                    t.wait_ge(sem("dXR%d" % (idx % 2)), (idx // 2 + 1) * 64)
                    for qt in range(4):
                        par = qt % 2
                        gq = idx * 4 + qt
                        if gq >= 2:
                            t.wait_ge(sem("vW"), gq - 1)
                        for dt in range(NDT):
                            t.matmul(pS[par][:, 0:256],
                                     a0b[:, 16384 * qsb + 512 * dt + 128 * qt:
                                         16384 * qsb + 512 * dt + 128 * (qt + 1)],
                                     slb[:, 8192 * (idx % 2) + 256 * dt:
                                         8192 * (idx % 2) + 256 * (dt + 1)],
                                     start=(dt == 0), stop=False)
                        inc(t.matmul(pS[par][:, 0:256], ones1b[:],
                                     wobt[0:1, 256 * (idx % 2):256 * (idx % 2) + 256],
                                     start=False, stop=True), "tW", 1)

    def ln2_scalar(s_, lb):
        pq = lb["pq"]
        ph1 = [a1f[:, 4096 * q:4096 * (q + 1)] for q in range(4)]
        phT0 = 16384 * pq
        for qt in range(4):
            s_.wait_ge(sem("vB"), lb["vB2"] + 8 * (qt + 1))
            if qt:
                s_.wait_ge(sem("sLN"), lb["sq2"] + qt)
            inc(s_.activation(
                a0b[:, phT0 + 4096 * qt:phT0 + 4096 * (qt + 1)],
                ph1[qt], AF.Square, bias=0.0, scale=1.0,
                accum_out=st4b[:, qt:qt + 1]), "sLN", 1)
        s_.wait_ge(sem("vLN"), lb["vLN9"])
        inc(s_.activation(st4c[:], st4b[:], AF.Ln, bias=0.0,
                          scale=1.0), "sLN", 1)
        s_.wait_ge(sem("sLN"), lb["sq2"] + 5)
        inc(s_.activation(st4b[:], st4c[:], AF.Exp, bias=0.0,
                          scale=-0.5), "sLN", 1)
        s_.wait_ge(sem("vLN"), lb["vLN11"])
        for qt in range(4):
            inc(s_.activation(ph1[qt], ph1[qt], AF.Identity,
                              bias=st4c[:, qt:qt + 1],
                              scale=st4b[:, qt:qt + 1]), "sOUT", 1)

    def ln2_vector(v, lb):
        for qt in range(4):
            v.wait_ge(sem("vB"), lb["vB2"] + 8 * (qt + 1))
            inc(v.reduce_sum(st4a[:, qt:qt + 1],
                             s1p[:, 16 * qt:16 * qt + 8], AX.X), "vLN", 1)
        v.wait_ge(sem("sLN"), lb["sq2"] + 4)
        v.wait_ge(sem("vLN"), cnt["vLN"])
        inc(v.tensor_scalar_mul(st4a[:], st4a[:], INV_D), "vLN", 1)
        v.wait_ge(sem("vLN"), cnt["vLN"])
        inc(v.tensor_scalar_mul(st4b[:], st4b[:], INV_D), "vLN", 1)
        v.wait_ge(sem("vLN"), cnt["vLN"])
        inc(v.tensor_tensor(st4c[:], st4a[:], st4a[:], ALU.mult), "vLN", 1)
        v.wait_ge(sem("vLN"), cnt["vLN"])
        inc(v.tensor_sub(st4b[:], st4b[:], st4c[:]), "vLN", 1)
        v.wait_ge(sem("vLN"), cnt["vLN"])
        inc(v.tensor_scalar_add(st4b[:], st4b[:], EPS), "vLN", 1)
        lb["vLN9"] = cnt["vLN"]
        v.wait_ge(sem("sLN"), lb["sq2"] + 6)
        inc(v.tensor_tensor(st4c[:], st4a[:], st4b[:], ALU.mult), "vLN", 1)
        v.wait_ge(sem("vLN"), cnt["vLN"])
        inc(v.tensor_scalar_mul(st4c[:], st4c[:], -1.0), "vLN", 1)
        lb["vLN11"] = cnt["vLN"]

    def ln2_store(sy, lb):
        pq = lb["pq"]
        ph1 = [a1f[:, 4096 * q:4096 * (q + 1)] for q in range(4)]
        for qt in range(4):
            sy.wait_ge(sem("sOUT"), lb["sOUT"] + qt + 1)
            inc(sy.dma_start(
                out=d_y[512 * pq + 128 * qt:512 * pq + 128 * (qt + 1), :],
                in_=ph1[qt]), "dY%d" % qt, 16)

    def emit_E_front(qsb):
        h1r = [a1f[:, 4096 * q:4096 * (q + 1)] for q in range(4)]
        hT0 = 16384 * qsb
        with nc.Block() as blk:
            vW_all = cnt["vW"]
            tW_all = cnt["tW"]
            sq_base = cnt["sLN"]
            tT2_base = cnt["tT2"]
            vDC_base = cnt["vDC"]

            @blk.scalar
            def _(s_):
                s_.wait_ge(sem("vW"), vW_all)
                s_.wait_ge(sem("tW"), tW_all)
                for qt in range(4):
                    if qt:
                        s_.wait_ge(sem("sLN"), cnt["sLN"])
                    inc(s_.activation(
                        slb[:, 4096 * (qt % 2):4096 * (qt % 2) + 4096],
                        h1r[qt], AF.Square, bias=0.0, scale=1.0,
                        accum_out=st4b[:, qt:qt + 1]), "sLN", 1)

            @blk.vector
            def _(v):
                v.wait_ge(sem("vW"), vW_all)
                for qt in range(4):
                    inc(v.reduce_sum(st4a[:, qt:qt + 1],
                                     s1p[:, 16 * qt:16 * qt + 16], AX.X),
                        "vLN", 1)
                v.wait_ge(sem("sLN"), sq_base + 4)
                v.wait_ge(sem("vLN"), cnt["vLN"])
                inc(v.tensor_scalar_mul(st4a[:], st4a[:], INV_D), "vLN", 1)
                v.wait_ge(sem("vLN"), cnt["vLN"])
                inc(v.tensor_scalar_mul(st4b[:], st4b[:], INV_D), "vLN", 1)
                v.wait_ge(sem("vLN"), cnt["vLN"])
                inc(v.tensor_tensor(st4c[:], st4a[:], st4a[:], ALU.mult),
                    "vLN", 1)
                v.wait_ge(sem("vLN"), cnt["vLN"])
                inc(v.tensor_sub(st4b[:], st4b[:], st4c[:]), "vLN", 1)
                v.wait_ge(sem("vLN"), cnt["vLN"])
                inc(v.tensor_scalar_add(st4b[:], st4b[:], EPS), "vLN", 1)

            @blk.scalar
            def _(s_):
                s_.wait_ge(sem("vLN"), cnt["vLN"])
                inc(s_.activation(st4c[:], st4b[:], AF.Ln, bias=0.0,
                                  scale=1.0), "sLN", 1)
                s_.wait_ge(sem("sLN"), cnt["sLN"])
                inc(s_.activation(st4b[:], st4c[:], AF.Exp, bias=0.0,
                                  scale=-0.5), "sLN", 1)

            @blk.vector
            def _(v):
                v.wait_ge(sem("sLN"), cnt["sLN"])
                inc(v.tensor_tensor(st4c[:], st4a[:], st4b[:], ALU.mult),
                    "vLN", 1)
                v.wait_ge(sem("vLN"), cnt["vLN"])
                inc(v.tensor_scalar_mul(st4c[:], st4c[:], -1.0), "vLN", 1)

            @blk.scalar
            def _(s_):
                s_.wait_ge(sem("vLN"), cnt["vLN"])
                for qt in range(4):
                    inc(s_.activation(h1r[qt], h1r[qt], AF.Identity,
                                      bias=st4c[:, qt:qt + 1],
                                      scale=st4b[:, qt:qt + 1]), "sLN", 1)

            @blk.tensor
            def _(t):
                for qt in range(4):
                    t.wait_ge(sem("vDC"), vDC_base + qt + 1)
                    for dtg in range(8):
                        gidx = cnt["tT2"]
                        par = gidx % 2
                        if gidx >= 2:
                            t.wait_ge(sem("vT2"), gidx - 1)
                        for j in range(4):
                            dt = 4 * dtg + j
                            ins = t.transpose(
                                pSb[par][:, 128 * j:128 * (j + 1)],
                                dcbuf[qt % 2][:, 128 * dt:128 * (dt + 1)],
                                idb[:])
                        inc(ins, "tT2", 1)

            @blk.vector
            def _(v):
                for qt in range(4):
                    v.wait_ge(sem("sLN"), sq_base + 7 + qt)
                    if qt < 2:
                        if tT2_base > 0:
                            v.wait_ge(sem("tT2"), tT2_base)
                    else:
                        v.wait_ge(sem("tT2"), tT2_base + 8 * (qt - 1))
                    inc(v.tensor_copy(dcbuf[qt % 2], h1r[qt]), "vDC", 1)
                    for dtg in range(8):
                        gidx = cnt["vT2"]
                        par = gidx % 2
                        v.wait_ge(sem("tT2"), gidx + 1)
                        dst = a0b[:, hT0:hT0 + 16384].rearrange(
                            "p (dt q) -> p dt q", q=512)[
                            :, 4 * dtg:4 * dtg + 4, 128 * qt:128 * (qt + 1)]
                        inc(v.tensor_copy(
                            dst, pSb[par][:, 0:512]
                                .rearrange("p (j q) -> p j q", j=4)),
                            "vT2", 1)

            @blk.gpsimd
            def _(g):
                g.wait_ge(sem("tW"), tW_all)
                inc(g.dma_start(out=slb[:, 8192:12288],
                                in_=din["bT"][0:128, :]), "dBT", 16)
                inc(g.dma_start(out=slb[:, 12288:16384],
                                in_=din["bT"][128:256, :]), "dBT", 16)
                g.wait_ge(sem("sLN"), sq_base + 4)
                inc(g.dma_start(
                    out=slb[:, 0:8192].rearrange("p (t f) -> p t f", t=NDT),
                    in_=din["aT"][:, 0:256]
                        .rearrange("(t p) f -> p t f", p=128)), "dAT", 16)
                esnap["dAT0"] = cnt["dAT"]
                g.wait_ge(sem("tT2"), tT2_base + 24)
                inc(g.dma_start(out=vS8[:, 0:4096],
                                in_=din["bT"][256:384, :]), "dBT", 16)
                g.wait_ge(sem("tT2"), tT2_base + 32)
                inc(g.dma_start(out=p_sb4[:, 0:4096],
                                in_=din["bT"][384:512, :]), "dBT", 16)
                esnap["dBT4"] = cnt["dBT"]

        with nc.Block() as blk:
            @blk.tensor
            def _(t):
                t.wait_ge(sem("vT2"), cnt["vT2"])
                for dt in range(NDT):
                    ins = t.matmul(pbB[0:16, 0:512],
                                   wqr_sb[:, 16 * dt:16 * (dt + 1)],
                                   a0b[:, hT0 + 512 * dt:hT0 + 512 * (dt + 1)],
                                   start=(dt == 0), stop=(dt == NDT - 1))
                inc(ins, "tR", 1)

            @blk.scalar
            def _(s_):
                s_.wait_ge(sem("tR"), cnt["tR"])
                inc(s_.copy(qr_sbT[:], pbB[0:16, 0:512]), "sR", 1)

            @blk.tensor
            def _(t):
                t.wait_ge(sem("sR"), cnt["sR"])
                for qt in range(4):
                    inc(t.matmul(pbA[:, 64 * qt:64 * (qt + 1)],
                                 qr_sbT[:, 128 * qt:128 * (qt + 1)], keysb[:],
                                 start=True, stop=True), "tSc", 1)

            @blk.vector
            def _(v):
                v.wait_ge(sem("tSc"), cnt["tSc"])
                inc(v.tensor_copy(sc_sb[:], pbA[:, 0:256]), "vSc", 1)

        for qt in range(4):
            with nc.Block() as blk:
                @blk.vector
                def _(v):
                    sc = sc_sb[:, 64 * qt:64 * (qt + 1)]
                    v.wait_ge(sem("vSc"), cnt["vSc"])
                    if qt == 0:
                        v.wait_ge(sem("sTK"), cnt["sTK"])
                    inc(v.max(out=mx_sb[:, 0:8], in_=sc), "vTK", 1)
                    v.wait_ge(sem("vTK"), cnt["vTK"])
                    inc(v.tensor_scalar(fl_sb[:], sc, mx_sb[:, 7:8], None,
                                        ALU.is_ge), "vTK", 1)
                    v.wait_ge(sem("vTK"), cnt["vTK"])
                    inc(v.scalar_tensor_tensor(zz_sb[:], fl_sb[:], -1e30, sc,
                                               ALU.mult, ALU.add), "vTK", 1)
                    v.wait_ge(sem("vTK"), cnt["vTK"])
                    inc(v.max(out=mx_sb[:, 8:16], in_=zz_sb[:]), "vTK", 1)
                    v.wait_ge(sem("vTK"), cnt["vTK"])
                    inc(v.tensor_scalar(msk_sb[:], sc, mx_sb[:, 15:16], None,
                                        ALU.is_ge), "vTK", 1)
                    inc(v.reduce_max(st_d[:], sc, AX.X, negate=True), "vTK", 1)

                @blk.scalar
                def _(s_):
                    s_.wait_ge(sem("vTK"), cnt["vTK"])
                    inc(s_.activation(ex_sb[:], sc_sb[:, 64 * qt:64 * (qt + 1)],
                                      AF.Exp, bias=st_d[:], scale=1.0), "sTK", 1)

                @blk.vector
                def _(v):
                    v.wait_ge(sem("sTK"), cnt["sTK"])
                    inc(v.scalar_tensor_tensor(ge_sb[:], msk_sb[:], 1.0, ex_sb[:],
                                               ALU.mult, ALU.mult,
                                               accum_out=st_d[:]), "vTK", 1)
                    v.wait_ge(sem("vTK"), cnt["vTK"])
                    inc(v.reciprocal(st_c[:], st_d[:]), "vTK", 1)
                    v.wait_ge(sem("vTK"), cnt["vTK"])
                    inc(v.tensor_scalar_mul(gate_sb[:], ge_sb[:], st_c[:]), "vTK", 1)

                @blk.tensor
                def _(t):
                    t.wait_ge(sem("vTK"), cnt["vTK"])
                    inc(t.transpose(pbAb[0:64, 1024:1152], gate_sb[:], idb[:]),
                        "tGT", 1)

                @blk.scalar
                def _(s_):
                    s_.wait_ge(sem("tGT"), cnt["tGT"])
                    inc(s_.copy(gT_sb[:], pbAb[0:64, 1024:1152]), "sGT", 1)

                @blk.tensor
                def _(t):
                    t.wait_ge(sem("sGT"), cnt["sGT"])
                    t.wait_ge(sem("dE8"), 16)
                    for m in range(4):
                        inc(t.matmul(pbB[:, 512 + 128 * m:512 + 128 * (m + 1)],
                                     e8b[:, 128 * m:128 * (m + 1)], gT_sb[:],
                                     start=True, stop=True), "tGX", 1)

                @blk.scalar
                def _(s_):
                    s_.wait_ge(sem("tGX"), cnt["tGX"])
                    dst = gx.rearrange("p (m q) -> p m q", m=4)[
                        :, :, 128 * qt:128 * (qt + 1)]
                    src = pbB[:, 512:1024].rearrange("p (m q) -> p m q", m=4)
                    inc(s_.copy(dst, src), "sGX", 1)

        with nc.Block() as blk:
            base_tZ = cnt["tZ"]

            @blk.gpsimd
            def _(g):
                g.wait_ge(sem("tZ"), base_tZ + 2)
                inc(g.dma_start(
                    out=slb[:, 0:8192].rearrange("p (t f) -> p t f", t=NDT),
                    in_=din["aT"][:, 256:512]
                        .rearrange("(t p) f -> p t f", p=128)), "dAT", 16)

            @blk.tensor
            def _(t):
                for ha in range(2):
                    t.wait_ge(sem("dAT"), esnap["dAT0"] + 16 * ha)
                    if ha == 0:
                        t.wait_ge(sem("sGX"), cnt["sGX"])
                    for mm_ in range(2):
                        m = 2 * ha + mm_
                        par = m % 2
                        if m >= 2:
                            t.wait_ge(sem("vZ"), base_tZ + m - 1)
                        for dt in range(NDT):
                            ins = t.matmul(pS[par],
                                           slb[:, 256 * dt + 128 * mm_:
                                               256 * dt + 128 * (mm_ + 1)],
                                           a0b[:, hT0 + 512 * dt:hT0 + 512 * (dt + 1)],
                                           start=(dt == 0), stop=(dt == NDT - 1))
                        inc(ins, "tZ", 1)

            @blk.vector
            def _(v):
                for m in range(4):
                    par = m % 2
                    v.wait_ge(sem("tZ"), base_tZ + m + 1)
                    inc(v.tensor_tensor(gx[:, 512 * m:512 * (m + 1)], pS[par],
                                        gx[:, 512 * m:512 * (m + 1)], ALU.mult),
                        "vZ", 1)

        for qt in range(4):
            with nc.Block() as blk:
                base_tB = cnt["tB"]
                base_vB = cnt["vB"]

                @blk.tensor
                def _(t):
                    t.wait_ge(sem("dBT"), esnap["dBT4"])
                    t.wait_ge(sem("vZ"), cnt["vZ"])
                    for oc in range(8):
                        par = oc % 2
                        if oc >= 2:
                            t.wait_ge(sem("vB"), base_vB + oc - 1)
                        elif base_vB > 0:
                            t.wait_ge(sem("vB"), base_vB - 1 + oc)
                        rhs = [slb[:, 8192 + 512 * oc:8192 + 512 * (oc + 1)],
                               slb[:, 12288 + 512 * oc:12288 + 512 * (oc + 1)],
                               vS8[:, 512 * oc:512 * (oc + 1)],
                               p_sb4[:, 512 * oc:512 * (oc + 1)]]
                        for m in range(4):
                            ins = t.matmul(
                                pS[par],
                                gx[:, 512 * m + 128 * qt:512 * m + 128 * (qt + 1)],
                                rhs[m],
                                start=(m == 0), stop=(m == 3))
                        inc(ins, "tB", 1)

                @blk.vector
                def _(v):
                    for oc in range(8):
                        par = oc % 2
                        v.wait_ge(sem("tB"), base_tB + oc + 1)
                        inc(v.scalar_tensor_tensor(
                            h1r[qt][:, 512 * oc:512 * (oc + 1)], pS[par], 2.0,
                            h1r[qt][:, 512 * oc:512 * (oc + 1)],
                            ALU.mult, ALU.add,
                            accum_out=s1p[:, 16 * qt + oc:16 * qt + oc + 1]),
                            "vB", 1)


    def emit_E_ln2(qsb):
        h1r = [a1f[:, 4096 * q:4096 * (q + 1)] for q in range(4)]
        hT0 = 16384 * qsb
        with nc.Block() as blk:
            vB2_base = cnt["vB"] - 32
            sq2_base = cnt["sLN"]
            sOUT_base = cnt["sOUT"]

            @blk.scalar
            def _(s_):
                for qt in range(4):
                    s_.wait_ge(sem("vB"), vB2_base + 8 * (qt + 1))
                    if qt:
                        s_.wait_ge(sem("sLN"), cnt["sLN"])
                    inc(s_.activation(
                        a0b[:, hT0 + 4096 * qt:hT0 + 4096 * (qt + 1)],
                        h1r[qt], AF.Square, bias=0.0, scale=1.0,
                        accum_out=st4b[:, qt:qt + 1]), "sLN", 1)

            @blk.vector
            def _(v):
                for qt in range(4):
                    v.wait_ge(sem("vB"), vB2_base + 8 * (qt + 1))
                    inc(v.reduce_sum(st4a[:, qt:qt + 1],
                                     s1p[:, 16 * qt:16 * qt + 8], AX.X),
                        "vLN", 1)
                v.wait_ge(sem("sLN"), sq2_base + 4)
                v.wait_ge(sem("vLN"), cnt["vLN"])
                inc(v.tensor_scalar_mul(st4a[:], st4a[:], INV_D), "vLN", 1)
                v.wait_ge(sem("vLN"), cnt["vLN"])
                inc(v.tensor_scalar_mul(st4b[:], st4b[:], INV_D), "vLN", 1)
                v.wait_ge(sem("vLN"), cnt["vLN"])
                inc(v.tensor_tensor(st4c[:], st4a[:], st4a[:], ALU.mult),
                    "vLN", 1)
                v.wait_ge(sem("vLN"), cnt["vLN"])
                inc(v.tensor_sub(st4b[:], st4b[:], st4c[:]), "vLN", 1)
                v.wait_ge(sem("vLN"), cnt["vLN"])
                inc(v.tensor_scalar_add(st4b[:], st4b[:], EPS), "vLN", 1)

            @blk.scalar
            def _(s_):
                s_.wait_ge(sem("vLN"), cnt["vLN"])
                inc(s_.activation(st4c[:], st4b[:], AF.Ln, bias=0.0,
                                  scale=1.0), "sLN", 1)
                s_.wait_ge(sem("sLN"), cnt["sLN"])
                inc(s_.activation(st4b[:], st4c[:], AF.Exp, bias=0.0,
                                  scale=-0.5), "sLN", 1)

            @blk.vector
            def _(v):
                v.wait_ge(sem("sLN"), cnt["sLN"])
                inc(v.tensor_tensor(st4c[:], st4a[:], st4b[:], ALU.mult),
                    "vLN", 1)
                v.wait_ge(sem("vLN"), cnt["vLN"])
                inc(v.tensor_scalar_mul(st4c[:], st4c[:], -1.0), "vLN", 1)

            @blk.scalar
            def _(s_):
                s_.wait_ge(sem("vLN"), cnt["vLN"])
                for qt in range(4):
                    inc(s_.activation(h1r[qt], h1r[qt], AF.Identity,
                                      bias=st4c[:, qt:qt + 1],
                                      scale=st4b[:, qt:qt + 1]), "sOUT", 1)

            @blk.sync
            def _(sy):
                for qt in range(4):
                    sy.wait_ge(sem("sOUT"), sOUT_base + qt + 1)
                    inc(sy.dma_start(
                        out=d_y[512 * qsb + 128 * qt:512 * qsb + 128 * (qt + 1), :],
                        in_=h1r[qt]), "dY%d" % qt, 16)

    emit_D(0, False)
    emit_E_front(0)
    emit_D(1, True)
    emit_E_front(1)
    emit_E_ln2(1)

    with nc.Block() as blk:
        @blk.sync
        def _(sy):
            for qt in range(4):
                sy.wait_ge(sem("dY%d" % qt), cnt["dY%d" % qt])

    return nc


def _host_inputs(L0_attn_x, L0_attn_wq_w, L0_attn_wq_b, L0_attn_wk_w,
                 L0_attn_wk_b, L0_attn_wv_w, L0_attn_wv_b, L0_attn_wo_w,
                 L0_attn_wo_b, L0_ffn_A_w, L0_ffn_B_w, L0_ffn_Wq_w,
                 L0_ffn_keys):
    x = np.ascontiguousarray(np.asarray(L0_attn_x, dtype=np.float32)[0])
    wqT = np.ascontiguousarray(np.asarray(L0_attn_wq_w, np.float32).T)
    wkT = np.ascontiguousarray(np.asarray(L0_attn_wk_w, np.float32).T)
    wvT = np.ascontiguousarray(np.asarray(L0_attn_wv_w, np.float32).T)
    woT = np.ascontiguousarray(np.asarray(L0_attn_wo_w, np.float32).T)
    aT = np.ascontiguousarray(np.asarray(L0_ffn_A_w, np.float32).T)
    bT = np.ascontiguousarray(np.asarray(L0_ffn_B_w, np.float32).T)
    wqrT = np.ascontiguousarray(np.asarray(L0_ffn_Wq_w, np.float32).T)
    keysT = np.ascontiguousarray(np.asarray(L0_ffn_keys, np.float32).T)
    wqb = np.ascontiguousarray(np.asarray(L0_attn_wq_b, np.float32)
                               .reshape(NH, 128).T)
    wkb = np.ascontiguousarray(np.asarray(L0_attn_wk_b, np.float32)
                               .reshape(128, 1))
    wvb = np.ascontiguousarray(np.asarray(L0_attn_wv_b, np.float32)
                               .reshape(128, 1))
    wob = np.ascontiguousarray(np.asarray(L0_attn_wo_b, np.float32)
                               .reshape(1, D))
    ident = np.eye(128, dtype=np.float32)
    jj = np.arange(896)[None, :]
    pp = np.arange(128)[:, None]
    tri = np.where(jj >= pp + 384, 0.0, -30000.0).astype(np.float32)
    tri = np.ascontiguousarray(tri)
    e8 = np.zeros((64, 512), np.float32)
    for n in range(512):
        e8[n // 8, n] = 1.0

    in_maps = []
    for c in range(8):
        j0, j1 = c, 15 - c
        xown = np.ascontiguousarray(
            np.concatenate([x[512 * j0:512 * (j0 + 1)],
                            x[512 * j1:512 * (j1 + 1)]], axis=0))
        tbl = np.zeros((1, 32), np.int32)
        for g in range(7):
            if g < c:
                tbl[0, g] = 512 * g
                tbl[0, 7 + g] = 0
            else:
                tbl[0, g] = 4096 + 512 * (g - c)
                tbl[0, 7 + g] = 512
        tbl[0, 14] = 512 * j0
        tbl[0, 15] = 512 * j1
        in_maps.append({
            "xfull": x, "xown": xown, "wqT": wqT, "wkT": wkT, "wvT": wvT,
            "woT": woT, "aT": aT, "bT": bT, "wqrT": wqrT, "keysT": keysT,
            "wqb": wqb, "wkb": wkb, "wvb": wvb, "wob": wob, "ident": ident,
            "tri": tri, "e8": e8, "tbl": tbl,
        })
    return in_maps


def kernel(**inputs):
    in_maps = _host_inputs(**inputs)
    if "nc" not in _CACHE:
        _CACHE["nc"] = build_nc()
    nc = _CACHE["nc"]
    res = run_bass_kernel_spmd(nc, in_maps, core_ids=list(range(8)))
    y = np.zeros((S, D), np.float32)
    for c in range(8):
        yc = res.results[c]["y"]
        y[512 * c:512 * (c + 1)] = yc[0:512]
        y[512 * (15 - c):512 * (16 - c)] = yc[512:1024]
    return y.reshape(1, S, D)


# version 14
# speedup vs baseline: 1.0124x; 1.0124x over previous
import math
import numpy as np
import concourse.bass as bass
import concourse.mybir as mybir
from concourse.bass import RegisterHandles
from concourse.expressions_rust import RuntimeValue
from concourse.bass_utils import run_bass_kernel_spmd

F32 = mybir.dt.float32
BF16 = mybir.dt.bfloat16
I32 = mybir.dt.int32
U8 = mybir.dt.uint8
AF = mybir.ActivationFunctionType
ALU = mybir.AluOpType
AX = mybir.AxisListType
PE = mybir.EngineType.PE
POOL = mybir.EngineType.Pool

S = 8192
D = 4096
NH = 32
QL = 1024
NDT = 32
SCALE = 1.0 / math.sqrt(128.0)
EPS = 1e-5
NE, BN, RD = 64, 512, 16
INV_D = 1.0 / 4096.0
NIT = 68
NSB = 18

_CACHE = {}
_DEBUG = False


def build_nc():
    nc = bass.Bass("TRN2", target_bir_lowering=False)
    cnt = {}
    sems = {}
    for name in ("dC", "dCg", "vC", "dX0", "dX1", "tTP", "vXT", "tKV", "sKV", "tTV", "vTV",
                 "dKV0", "dKV1", "dWQ0", "dWQ1", "tQ", "sQ", "dST", "semS", "semP", "semT",
                 "tST", "sLE", "dRC0", "dRC1", "dRB0", "dRB1", "vAE", "vAEr", "vAC", "tAV", "dWO0", "dWO1", "dXR0", "dXR1", "tW", "vW",
                 "vLN", "sLN", "vDC", "tT2", "vT2", "tR", "sR", "tSc", "vSc",
                 "vTK", "sTK", "tGT", "sGT", "tGX", "sGX", "dAT", "tZ", "vZ",
                 "dBT", "tB", "vB", "sOUT", "dY0", "dY1", "dY2", "dY3", "dE8", "dDBG", "vQC", "dWC", "vPS"):
        sems[name] = nc.alloc_semaphore(name)
        cnt[name] = 0

    def sem(name):
        return sems[name]

    def inc(ins, name, n):
        ins.then_inc(sems[name], n)
        cnt[name] += n
        return ins

    din = {}
    for nm, shp in (("xfull", [S, D]), ("xown", [QL, D]), ("wqT", [D, D]),
                    ("wkT", [D, 128]), ("wvT", [D, 128]), ("woT", [D, D]),
                    ("aT", [D, BN]), ("bT", [BN, D]), ("wqrT", [D, RD]),
                    ("keysT", [RD, NE]), ("wqb", [128, NH]), ("wkb", [128, 1]),
                    ("wvb", [128, 1]), ("wob", [1, D]), ("ident", [128, 128]),
                    ("tri", [128, 896]), ("e8", [64, 512])):
        din[nm] = nc.dram_tensor(nm, shp, F32, kind="ExternalInput").ap()
    d_tbl = nc.dram_tensor("tbl", [1, 32], I32, kind="ExternalInput").ap()
    d_y = nc.dram_tensor("y", [QL, D], F32, kind="ExternalOutput").ap()
    d_kT = nc.dram_tensor("kT_scr", [128, S], BF16, kind="Internal").ap()
    d_v = nc.dram_tensor("v_scr", [128, S], BF16, kind="Internal").ap()
    d_rcp = nc.dram_tensor("rcp_scr", [2, 1024], F32, kind="Internal").ap()
    d_wob = nc.dram_tensor("wo_scr", [D, D], BF16, kind="Internal").ap()
    d_wobb = nc.dram_tensor("wob_scr", [1, D], BF16, kind="Internal").ap()
    if _DEBUG:
        d_dbgq = nc.dram_tensor("dbgq", [128, 32768], BF16, kind="ExternalOutput").ap()
        d_dbga = nc.dram_tensor("dbga", [128, 32768], BF16, kind="ExternalOutput").ap()

    A = nc.alloc_sbuf_tensor
    kT8 = A("kT8", [128, 4096], BF16)
    vS8 = A("vS8", [128, 4096], BF16)
    arena0 = A("arena0", [128, 65536], U8)
    arena1 = A("arena1", [128, 65536], U8)
    slab = A("slab", [128, 49152], U8)
    idb = A("idb", [128, 128], BF16)
    onesb = A("onesb", [128, 1], BF16)
    onesf = A("onesf", [128, 1], F32)
    ones1b = A("ones1b", [1, 128], BF16)
    ones1f = A("ones1f", [1, 128], F32)
    wqb_sb = A("wqb_sb", [128, NH], F32)
    wkb_sb = A("wkb_sb", [128, 1], F32)
    wvb_sb = A("wvb_sb", [128, 1], F32)
    wobt = A("wobt", [1, 512], BF16)
    keysb = A("keysb", [RD, NE], BF16)
    wqr_sb = A("wqr_sb", [128, NDT * RD], BF16)
    tbl_sb = A("tbl_sb", [1, 32], I32)
    s1p = A("s1p", [128, 64], F32)
    st_a = A("st_a", [128, 1], F32)
    st_b = A("st_b", [128, 1], F32)
    st_c = A("st_c", [128, 1], F32)
    st_d = A("st_d", [128, 1], F32)
    st4a = A("st4a", [128, 4], F32)
    st4b = A("st4b", [128, 4], F32)
    st4c = A("st4c", [128, 4], F32)
    mx_sb = A("mx_sb", [128, 16], F32)
    gate_sb = A("gate_sb", [128, 64], BF16)
    gT_sb = A("gT_sb", [64, 128], BF16)
    qr_sbT = A("qr_sbT", [RD, 512], BF16)
    qcur = A("qcur", [128, 1024], BF16)
    p_sb4 = A("p_sb4", [128, 4096], BF16)

    a0b = arena0.ap().bitcast(BF16)
    a0f = arena0.ap().bitcast(F32)
    a1b = arena1.ap().bitcast(BF16)
    a1f = arena1.ap().bitcast(F32)
    slb = slab.ap().bitcast(BF16)
    slf = slab.ap().bitcast(F32)

    wkTb = a1b[:, 16384:20480]
    wvTb = a1b[:, 20480:24576]
    ktmp = slb[:, 16384:17408]
    vtt = slb[:, 17408:18432]
    vtmp = slb[:, 18432:19456]
    kstage = slb[:, 16384:19968]
    vstage = slb[:, 19968:23552]
    kdiag = slb[:, 23552:24576]
    trib = slb[:, 0:896]
    zerb = slb[:, 896:1408]
    vdiag = slb[:, 2432:3456]
    rcp_sb = [slf[0:1, 1728:2752], slf[0:1, 2752:3776]]
    rb_sb = [slf[:, 3776:4800], slf[:, 4800:5824]]
    acc_sb = slf[:, 5824:6848]
    acc0_sb = slf[:, 6848:7872]
    ptmp = slb[:, 15744:16256]
    xres = slf[:, 8192:10240]
    gx = kT8.ap()[:, 0:2048]
    e8b = kT8.ap()[0:64, 2048:2560]
    kf = kT8.ap().bitcast(F32)
    sc_sb = kf[:, 1280:1536]
    fl_sb = kf[:, 1536:1600]
    zz_sb = kf[:, 1600:1664]
    msk_sb = kf[:, 1664:1728]
    ex_sb = kf[:, 1728:1792]
    ge_sb = kf[:, 1792:1856]

    pSA = nc.alloc_psum_tensor("pSA", [128, 2048], F32).ap()
    pA0 = nc.alloc_psum_tensor("pA0", [128, 1024], F32).ap()
    pZ = nc.alloc_psum_tensor("pZ", [128, 1024], F32).ap()
    pSum2 = pZ[0:1, 0:1024]
    pbA = pA0
    pbB = pSA[:, 1024:2048]
    pbSum = pSum2
    pS = [pSA[:, 0:512], pSA[:, 512:1024]]
    pSpair = [pSA[:, 0:1024], pSA[:, 1024:2048]]
    pSh = [[pSA[:, 0:512], pSA[:, 512:1024]],
           [pSA[:, 1024:1536], pSA[:, 1536:2048]]]
    pSAb = pSA.bitcast(BF16)
    pSb = [pSAb[:, 0:1024], pSAb[:, 1024:2048]]
    pbBb = pSAb[:, 2048:4096]
    pbAb = pA0.bitcast(BF16)

    with nc.Block() as blk:
        @blk.sync
        def _(sy):
            for dst, src in ((wqb_sb, din["wqb"]), (wkb_sb, din["wkb"]),
                             (wvb_sb, din["wvb"])):
                inc(sy.dma_start(out=dst[:], in_=src), "dC", 16)
            inc(sy.dma_start(out=tbl_sb[:], in_=d_tbl), "dC", 16)

        @blk.gpsimd
        def _(g):
            inc(g.dma_start(out=idb[:], in_=din["ident"]), "dCg", 16)
            inc(g.dma_start(out=keysb[:], in_=din["keysT"]), "dCg", 16)
            inc(g.dma_start(
                out=wqr_sb[:].rearrange("p (t f) -> p t f", t=NDT),
                in_=din["wqrT"].rearrange("(t p) f -> p t f", p=128)), "dCg", 16)
            inc(g.dma_start(
                out=wkTb[:].rearrange("p (t f) -> p t f", t=NDT),
                in_=din["wkT"].rearrange("(t p) f -> p t f", p=128)), "dCg", 16)
            inc(g.dma_start(
                out=wvTb[:].rearrange("p (t f) -> p t f", t=NDT),
                in_=din["wvT"].rearrange("(t p) f -> p t f", p=128)), "dCg", 16)

        @blk.vector
        def _(v):
            v.memset(onesb[:], 1.0)
            v.memset(onesf[:], 1.0)
            v.memset(ones1b[:], 1.0)
            inc(v.memset(ones1f[:], 1.0), "vC", 1)

    def xbuf(sb):
        return slb if (sb % 2 == 0 or sb >= 16) else a0b

    with nc.Block() as blk:
        @blk.gpsimd
        def _(g):
            for sb in range(NSB):
                if sb >= 2:
                    g.wait_ge(sem("tTP"), ((sb - 1) if sb < 16 else sb) * 32)
                src = din["xfull"] if sb < 16 else din["xown"]
                r0 = 512 * sb if sb < 16 else 512 * (sb - 16)
                for rt in range(4):
                    inc(g.dma_start(
                        out=xbuf(sb)[:, 4096 * rt:4096 * (rt + 1)],
                        in_=src[r0 + 128 * rt:r0 + 128 * (rt + 1), :]),
                        "dX%d" % (sb % 2), 16)

        @blk.tensor
        def _(t):
            t.wait_ge(sem("dC"), cnt["dC"])
            t.wait_ge(sem("dCg"), cnt["dCg"])
            t.wait_ge(sem("vC"), cnt["vC"])
            for sb in range(NSB):
                t.wait_ge(sem("dX%d" % (sb % 2)), (sb // 2 + 1) * 64)
                for dt in range(NDT):
                    gidx = sb * 32 + dt
                    par = gidx % 2
                    if gidx >= 2:
                        t.wait_ge(sem("vXT"), gidx - 1)
                    for rt in range(4):
                        ins = t.transpose(
                            pSb[par][:, 128 * rt:128 * (rt + 1)],
                            xbuf(sb)[:, 4096 * rt + 128 * dt:4096 * rt + 128 * (dt + 1)],
                            idb[:])
                    inc(ins, "tTP", 1)
                if sb < 16:
                    t.wait_ge(sem("vXT"), (sb + 1) * 32)
                    if sb > 0:
                        t.wait_ge(sem("sKV"), sb * 2)
                    for dt in range(NDT):
                        t.matmul(pbA[:, 0:512], wkTb[:, 128 * dt:128 * (dt + 1)],
                                 a1b[:, 512 * dt:512 * (dt + 1)],
                                 start=(dt == 0), stop=(dt == NDT - 1))
                    for dt in range(NDT):
                        ins = t.matmul(pbA[:, 512:1024], wvTb[:, 128 * dt:128 * (dt + 1)],
                                       a1b[:, 512 * dt:512 * (dt + 1)],
                                       start=(dt == 0), stop=(dt == NDT - 1))
                    inc(ins, "tKV", 1)
                    t.wait_ge(sem("sKV"), sb * 2 + 2)
                    if sb > 0:
                        t.wait_ge(sem("vTV"), sb)
                    for j in range(4):
                        ins = t.transpose(
                            pbBb[:, 128 * j:128 * (j + 1)],
                            vtt[:, 512 * (sb % 2) + 128 * j:512 * (sb % 2) + 128 * (j + 1)],
                            idb[:])
                    inc(ins, "tTV", 1)

        @blk.scalar
        def _(s_):
            for sb in range(16):
                s_.wait_ge(sem("tKV"), sb + 1)
                if sb >= 2:
                    s_.wait_ge(sem("dKV%d" % (sb % 2)), (sb // 2) * 32)
                kdst = kT8[:, 512 * sb:512 * (sb + 1)] if sb < 8 \
                    else ktmp[:, 512 * (sb % 2):512 * (sb % 2) + 512]
                inc(s_.activation(kdst, pbA[:, 0:512], AF.Identity,
                                  bias=wkb_sb[:], scale=1.0), "sKV", 1)
                inc(s_.activation(vtt[:, 512 * (sb % 2):512 * (sb % 2) + 512],
                                  pbA[:, 512:1024], AF.Identity,
                                  bias=wvb_sb[:], scale=1.0), "sKV", 1)

        @blk.vector
        def _(v):
            for sb in range(NSB):
                for dt in range(NDT):
                    gidx = sb * 32 + dt
                    par = gidx % 2
                    v.wait_ge(sem("tTP"), gidx + 1)
                    if 0 < sb < 16 and dt == 0:
                        v.wait_ge(sem("tKV"), sb)
                    if sb < 16:
                        dst = a1b[:, 512 * dt:512 * (dt + 1)]
                    else:
                        c0 = 1024 * dt + 512 * (sb - 16)
                        dst = a0b[:, c0:c0 + 512]
                    inc(v.tensor_copy(dst, pSb[par][:, 0:512]), "vXT", 1)
                if sb < 16:
                    v.wait_ge(sem("tTV"), sb + 1)
                    vdst2 = vS8[:, 512 * sb:512 * (sb + 1)] if sb < 8 \
                        else vtmp[:, 512 * (sb % 2):512 * (sb % 2) + 512]
                    inc(v.tensor_copy(vdst2, pbBb[:, 0:512]), "vTV", 1)

        @blk.sync
        def _(sy):
            for sb in range(16):
                sy.wait_ge(sem("sKV"), sb * 2 + 1)
                ksrc = kT8[:, 512 * sb:512 * (sb + 1)] if sb < 8 \
                    else ktmp[:, 512 * (sb % 2):512 * (sb % 2) + 512]
                inc(sy.dma_start(out=d_kT[:, 512 * sb:512 * (sb + 1)], in_=ksrc),
                    "dKV%d" % (sb % 2), 16)
                sy.wait_ge(sem("vTV"), sb + 1)
                vsrc = vS8[:, 512 * sb:512 * (sb + 1)] if sb < 8 \
                    else vtmp[:, 512 * (sb % 2):512 * (sb % 2) + 512]
                inc(sy.dma_start(out=d_v[:, 512 * sb:512 * (sb + 1)], in_=vsrc),
                    "dKV%d" % (sb % 2), 16)

    with nc.Block() as blk:
        @blk.gpsimd
        def _(g):
            for hp in range(16):
                if hp >= 2:
                    g.wait_ge(sem("tQ"), (hp - 1) * 4)
                else:
                    g.wait_ge(sem("tTP"), NSB * 32)
                inc(g.dma_start(
                    out=slb[:, 8192 * (hp % 2):8192 * (hp % 2) + 8192]
                        .rearrange("p (t f) -> p t f", t=NDT),
                    in_=din["wqT"][:, 256 * hp:256 * (hp + 1)]
                        .rearrange("(t p) f -> p t f", p=128)),
                    "dWQ%d" % (hp % 2), 16)

        @blk.tensor
        def _(t):
            t.wait_ge(sem("vXT"), NSB * 32)
            for hp in range(16):
                t.wait_ge(sem("dWQ%d" % (hp % 2)), (hp // 2 + 1) * 16)
                for hh in range(2):
                    h = 2 * hp + hh
                    for qh in range(2):
                        m = hp * 4 + hh * 2 + qh
                        if m >= 2:
                            t.wait_ge(sem("sQ"), m - 1)
                        for dt in range(NDT):
                            ins = t.matmul(
                                pbA[:, 512 * qh:512 * (qh + 1)],
                                slb[:, 8192 * (hp % 2) + 256 * dt + 128 * hh:
                                    8192 * (hp % 2) + 256 * dt + 128 * (hh + 1)],
                                a0b[:, 1024 * dt + 512 * qh:1024 * dt + 512 * (qh + 1)],
                                start=(dt == 0), stop=(dt == NDT - 1))
                        inc(ins, "tQ", 1)

        @blk.scalar
        def _(s_):
            for hp in range(16):
                for hh in range(2):
                    h = 2 * hp + hh
                    for qh in range(2):
                        m = hp * 4 + hh * 2 + qh
                        s_.wait_ge(sem("tQ"), m + 1)
                        inc(s_.activation(
                            a1b[:, 1024 * h + 512 * qh:1024 * h + 512 * (qh + 1)],
                            pbA[:, 512 * qh:512 * (qh + 1)], AF.Identity,
                            bias=wqb_sb[:, h:h + 1], scale=1.0), "sQ", 1)

    with nc.Block() as blk:
        @blk.gpsimd
        def _(g):
            g.wait_ge(sem("dKV0"), 256)
            g.wait_ge(sem("dKV1"), 256)
            g.wait_ge(sem("tQ"), NH * 2)
            inc(g.dma_start(out=trib, in_=din["tri"]), "dST", 16)
            _, vals = nc.values_load_multi_w_load_instructions(
                tbl_sb[0:1, 0:16], engines=[POOL],
                min_val=0, max_val=S - 512,
                skip_runtime_bounds_check=True)
            for gi in range(7):
                inc(g.dma_start(out=kstage[:, 512 * gi:512 * (gi + 1)],
                                in_=d_kT[:, bass.ds(vals[gi], 512)]), "dST", 16)
                inc(g.dma_start(out=vstage[:, 512 * gi:512 * (gi + 1)],
                                in_=d_v[:, bass.ds(vals[gi], 512)]), "dST", 16)
            for si in range(2):
                inc(g.dma_start(out=kdiag[:, 512 * si:512 * (si + 1)],
                                in_=d_kT[:, bass.ds(vals[14 + si], 512)]), "dST", 16)
                inc(g.dma_start(out=vdiag[:, 512 * si:512 * (si + 1)],
                                in_=d_v[:, bass.ds(vals[14 + si], 512)]), "dST", 16)

        @blk.vector
        def _(v):
            v.wait_ge(sem("tQ"), NH * 2)
            inc(v.memset(zerb, 0.0), "vC", 1)

    with nc.Block() as blk:
        @blk.gpsimd
        def _(g):
            for og in range(16):
                g.wait_ge(sem("semT"), (NIT // 2) * ((3 * og) // 2))
                inc(g.dma_start(out=d_wob[:, 256 * og:256 * (og + 1)],
                                in_=din["woT"][:, 256 * og:256 * (og + 1)]),
                    "dWC", 16)
            inc(g.dma_start(out=d_wobb[:], in_=din["wob"]), "dWC", 16)

        if _DEBUG:
            @blk.sync
            def _(sy):
                sy.wait_ge(sem("sQ"), NH * 2)
                inc(sy.dma_start(out=d_dbgq, in_=a1b[:, 0:32768]), "dDBG", 16)

        @blk.tensor
        def _(t):
            t.wait_ge(sem("dST"), cnt["dST"])
            t.wait_ge(sem("vC"), cnt["vC"])
            t.wait_ge(sem("sQ"), cnt["sQ"])
            qreg = [nc.alloc_register(PE, f"qoff{g}") for g in range(7)]
            for g in range(7):
                t.reg_load(qreg[g], tbl_sb[0:1, 7 + g:8 + g])
            qw = [RuntimeValue(RegisterHandles((qreg[g],)), 0, 512)
                  for g in range(7)]
            NP = NIT // 2

            def emit_fold(hh):
                t.wait_ge(sem("vAC"), 20 * (hh + 1))
                t.matmul(pSum2[0:1, 512:1024], onesf[:], acc_sb[:, 0:512],
                         start=False, stop=False)
                t.matmul(pSum2[0:1, 0:512], onesf[:], acc0_sb[:, 0:512],
                         start=False, stop=False)
                t.matmul(pSum2[0:1, 0:512], onesf[:], acc0_sb[:, 512:1024],
                         start=False, stop=True)
                inc(t.matmul(pSum2[0:1, 512:1024], onesf[:],
                             acc_sb[:, 512:1024], start=False, stop=True),
                    "tST", 1)

            for h in range(NH):
                bP = h * NP
                pA = pA0
                t.wait_ge(sem("vQC"), h + 1)
                plan = []
                for i in range(NIT):
                    par = i % 2
                    if i < 32:
                        lhsK = kT8[:, 128 * i:128 * (i + 1)]
                        lhsV = vS8[:, 128 * i:128 * (i + 1)]
                        qrhs = qcur[:, 512:1024]
                        aout = pA[:, 512:1024]
                        sout = pSum2[0:1, 512:1024]
                        diag = False
                    elif i < 60:
                        gi, j = (i - 32) // 4, (i - 32) % 4
                        lhsK = kstage[:, 512 * gi + 128 * j:512 * gi + 128 * (j + 1)]
                        lhsV = vstage[:, 512 * gi + 128 * j:512 * gi + 128 * (j + 1)]
                        qrhs = qcur[:, bass.ds(qw[gi], 512)]
                        aout = pA[:, bass.ds(qw[gi], 512)]
                        sout = pSum2[0:1, bass.ds(qw[gi], 512)]
                        diag = False
                    else:
                        si, dl = (i - 60) // 4, (i - 60) % 4
                        lhsK = kdiag[:, 512 * si + 128 * dl:512 * si + 128 * (dl + 1)]
                        lhsV = vdiag[:, 512 * si + 128 * dl:512 * si + 128 * (dl + 1)]
                        qrhs = qcur[:, 512 * si:512 * (si + 1)]
                        aout = pA[:, 512 * si:512 * (si + 1)]
                        sout = pSum2[0:1, 512 * si:512 * (si + 1)]
                        diag = True
                    plan.append((par, diag, dl if diag else 0, lhsK, lhsV,
                                 qrhs, aout, sout))

                def emit_spair(p):
                    sl = pSh[p % 2]
                    for j in (2 * p, 2 * p + 1):
                        parj, diagj, dlj, lhsKj, _, qrhsj, _, _ = plan[j]
                        if diagj:
                            t.matmul(sl[parj], idb[:],
                                     trib[:, 384 - 128 * dlj:896 - 128 * dlj],
                                     start=True, stop=False)
                            ins = t.matmul(sl[parj], lhsKj, qrhsj, start=False,
                                           stop=True)
                        else:
                            ins = t.matmul(sl[parj], lhsKj, qrhsj, start=True,
                                           stop=True)
                    inc(ins, "semS", 1)

                if h > 0:
                    t.wait_ge(sem("semP"), bP - 1)
                emit_spair(0)
                if h > 0:
                    t.wait_ge(sem("semP"), bP)
                emit_spair(1)
                if h > 0:
                    emit_fold(h - 1)
                t.wait_ge(sem("vAEr"), h)
                t.matmul(pA[:, 0:512], idb[:], zerb, start=True, stop=False)
                t.matmul(pA[:, 512:1024], idb[:], zerb, start=True, stop=False)
                for p in range(NP):
                    t.wait_ge(sem("semP"), bP + p + 1)
                    if p == 16:
                        t.wait_ge(sem("sLE"), 2 * h)
                        t.matmul(pSum2[0:1, 0:512], onesb[:], zerb,
                                 start=True, stop=False)
                        t.matmul(pSum2[0:1, 512:1024], onesb[:], zerb,
                                 start=True, stop=False)
                    for i in (2 * p, 2 * p + 1):
                        par, _, _, _, lhsV, _, aout, sout = plan[i]
                        pcol = 512 * (i % 8)
                        ins = t.matmul(aout, lhsV, p_sb4[:, pcol:pcol + 512],
                                       start=False, stop=False)
                    if 16 <= p < 30:
                        t.wait_ge(sem("vPS"), h * 14 + (p - 16) + 1)
                        ins = t.matmul(plan[2 * p][7], onesb[:], ptmp[:],
                                       start=False, stop=False)
                    inc(ins, "semT", 1)
                    if p + 2 < NP:
                        emit_spair(p + 2)
                t.matmul(pA[:, 0:512], idb[:], zerb, start=False, stop=True)
                inc(t.matmul(pA[:, 512:1024], idb[:], zerb, start=False,
                             stop=True), "tAV", 1)
            emit_fold(NH - 1)

        @blk.scalar
        def _(s_):
            NP = NIT // 2

            def emit_lnexp(hh):
                rp = hh % 2
                s_.wait_ge(sem("tST"), hh + 1)
                if hh >= 2:
                    s_.wait_ge(sem("dRC0" if hh % 2 == 0 else "dRC1"),
                               16 * (hh // 2))
                inc(s_.activation(rcp_sb[rp], pSum2[0:1, :], AF.Ln,
                                  bias=0.0, scale=1.0), "sLE", 1)
                s_.wait_ge(sem("sLE"), 2 * hh + 1)
                inc(s_.activation(rcp_sb[rp], rcp_sb[rp], AF.Exp,
                                  bias=0.0, scale=-1.0), "sLE", 1)

            for h in range(NH):
                bS = h * NP
                bT = h * NP
                for p in range(NP):
                    if p == 2 and h > 0:
                        emit_lnexp(h - 1)
                    s_.wait_ge(sem("semS"), bS + p + 1)
                    if bT + p - 3 > 0:
                        s_.wait_ge(sem("semT"), bT + p - 3)
                    if p < 4:
                        if h > 0:
                            s_.wait_ge(sem("vAC"), 20 * (h - 1) + 17 + p)
                    elif p <= 19:
                        s_.wait_ge(sem("vAC"), 20 * h + p - 3)
                    else:
                        s_.wait_ge(sem("vPS"), h * 14 + p - 19)
                    pcol = 1024 * (p % 4)
                    inc(s_.activation(p_sb4[:, pcol:pcol + 1024],
                                      pSpair[p % 2], AF.Exp,
                                      bias=0.0, scale=SCALE), "semP", 1)
            emit_lnexp(NH - 1)

        @blk.sync
        def _(sy):
            for h in range(NH):
                rp = h % 2
                j = h // 2
                sy.wait_ge(sem("sLE"), 2 * h + 2)
                inc(sy.dma_start(out=d_rcp[rp:rp + 1, :], in_=rcp_sb[rp]),
                    "dRC0" if rp == 0 else "dRC1", 16)
                sy.wait_ge(sem("dRC0" if rp == 0 else "dRC1"), 16 * (j + 1))
                if h >= 2:
                    sy.wait_ge(sem("vAE"), 2 * (h - 1))
                inc(sy.dma_start(out=rb_sb[rp],
                                 in_=d_rcp[rp:rp + 1, :].partition_broadcast(128)),
                    "dRB0" if rp == 0 else "dRB1", 16)

        @blk.vector
        def _(v):
            NP = NIT // 2

            def emit_norm(hh):
                rp = hh % 2
                v.wait_ge(sem("vAEr"), hh + 1)
                v.wait_ge(sem("dRB0" if hh % 2 == 0 else "dRB1"),
                          16 * (hh // 2 + 1))
                for si in range(2):
                    sl = a0b[:, 16384 * si + 512 * hh:
                             16384 * si + 512 * (hh + 1)]
                    inc(v.tensor_tensor(sl, sl,
                                        rb_sb[rp][:, 512 * si:512 * (si + 1)],
                                        ALU.mult), "vAE", 1)

            for h in range(NH):
                if h == 0:
                    v.wait_ge(sem("sQ"), NH * 2)
                else:
                    v.wait_ge(sem("semS"), NP * h)
                inc(v.tensor_copy(qcur[:], a1b[:, 1024 * h:1024 * (h + 1)]),
                    "vQC", 1)
                for p in range(NP):
                    if 16 <= p < 30:
                        v.wait_ge(sem("semP"), h * NP + p + 1)
                        v.wait_ge(sem("semT"), h * NP + p)
                        pcol = 1024 * (p % 4)
                        inc(v.tensor_tensor(
                            ptmp[:],
                            p_sb4[:, pcol:pcol + 512],
                            p_sb4[:, pcol + 512:pcol + 1024], ALU.add),
                            "vPS", 1)
                        continue
                    v.wait_ge(sem("semP"), h * NP + p + 1)
                    pcol = 1024 * (p % 4)
                    dst = acc0_sb if p in (30, 31) else acc_sb
                    if p in (0, 30):
                        if p == 0 and h > 0:
                            v.wait_ge(sem("tST"), h)
                        inc(v.tensor_copy(dst, p_sb4[:, pcol:pcol + 1024]),
                            "vAC", 1)
                    else:
                        v.wait_ge(sem("vAC"), cnt["vAC"])
                        inc(v.tensor_tensor(dst, dst,
                                            p_sb4[:, pcol:pcol + 1024], ALU.add),
                            "vAC", 1)
                    if p == 10 and h >= 2:
                        emit_norm(h - 2)
                v.wait_ge(sem("tAV"), h + 1)
                for si in range(2):
                    ins = v.tensor_copy(
                        a0b[:, 16384 * si + 512 * h:16384 * si + 512 * (h + 1)],
                        pA0[:, 512 * si:512 * (si + 1)])
                inc(ins, "vAEr", 1)
            emit_norm(NH - 2)
            emit_norm(NH - 1)

    dcbuf = [vS8[:, 0:4096], p_sb4[:, 0:4096]]
    esnap = {}

    def emit_D(qsb, prev_ln2):
        tZ_prev = cnt["tZ"]
        tB_prev = cnt["tB"]
        vW_prev = cnt["vW"]
        vB_prev = cnt["vB"]
        vZ_prev = cnt["vZ"]
        if prev_ln2:
            lb = {"vB2": cnt["vB"] - 32, "sq2": cnt["sLN"],
                  "sOUT": cnt["sOUT"], "pq": qsb - 1}
        with nc.Block() as blk:
            @blk.gpsimd
            def _(g):
                if qsb == 0:
                    g.wait_ge(sem("vAE"), 2 * NH)
                    g.wait_ge(sem("dRB0"), 16 * (NH // 2))
                    g.wait_ge(sem("dRB1"), 16 * (NH // 2))
                    g.wait_ge(sem("dWC"), 17 * 16)
                    inc(g.dma_start(out=e8b, in_=din["e8"]), "dE8", 16)
                for og in range(16):
                    idx = 16 * qsb + og
                    if idx == 16:
                        g.wait_ge(sem("tZ"), tZ_prev)
                    if idx == 17:
                        g.wait_ge(sem("tB"), tB_prev)
                    if idx >= 2:
                        g.wait_ge(sem("tW"), (idx - 1) * 4)
                    inc(g.dma_start(
                        out=slb[:, 8192 * (idx % 2):8192 * (idx % 2) + 8192]
                            .rearrange("p (t f) -> p t f", t=NDT),
                        in_=d_wob[:, 256 * og:256 * (og + 1)]
                            .rearrange("(t p) f -> p t f", p=128)),
                        "dWO%d" % (idx % 2), 16)
                    inc(g.dma_start(
                        out=wobt[0:1, 256 * (idx % 2):256 * (idx % 2) + 256],
                        in_=d_wobb[0:1, 256 * og:256 * (og + 1)]),
                        "dWO%d" % (idx % 2), 16)

            @blk.sync
            def _(sy):
                def xres_og(og):
                    idx = 16 * qsb + og
                    if idx >= 2:
                        sy.wait_ge(sem("vW"), (idx - 1) * 4)
                    elif idx == 0:
                        sy.wait_ge(sem("vAE"), 2 * NH)
                        sy.wait_ge(sem("dRB0"), 16 * (NH // 2))
                        sy.wait_ge(sem("dRB1"), 16 * (NH // 2))
                    if idx == 16:
                        sy.wait_ge(sem("vW"), vW_prev)
                    for qt in range(4):
                        inc(sy.dma_start(
                            out=xres[:, 1024 * (idx % 2) + 256 * qt:
                                     1024 * (idx % 2) + 256 * (qt + 1)],
                            in_=din["xown"][512 * qsb + 128 * qt:
                                            512 * qsb + 128 * (qt + 1),
                                            256 * og:256 * (og + 1)]),
                            "dXR%d" % (idx % 2), 16)

                if prev_ln2:
                    xres_og(0)
                    xres_og(1)
                    ln2_store(sy, lb)
                    for og in range(2, 16):
                        xres_og(og)
                else:
                    for og in range(16):
                        xres_og(og)

            if _DEBUG and qsb == 0:
                @blk.sync
                def _(sy):
                    sy.wait_ge(sem("vAE"), 2 * NH)
                    inc(sy.dma_start(out=d_dbga, in_=a0b[:, 0:32768]), "dDBG", 16)

            @blk.vector
            def _(v):
                if prev_ln2:
                    ln2_vector(v, lb)
                for og in range(16):
                    idx = 16 * qsb + og
                    for qt in range(4):
                        par = qt % 2
                        gq = idx * 4 + qt
                        if qsb == 1 and og == 0:
                            v.wait_ge(sem("dY%d" % qt), 16)
                        v.wait_ge(sem("tW"), gq + 1)
                        inc(v.scalar_tensor_tensor(
                            a1f[:, 4096 * qt + 256 * og:4096 * qt + 256 * (og + 1)],
                            pS[par][:, 0:256], 1.0,
                            xres[:, 1024 * (idx % 2) + 256 * qt:
                                 1024 * (idx % 2) + 256 * (qt + 1)],
                            ALU.mult, ALU.add,
                            accum_out=s1p[:, 16 * qt + og:16 * qt + og + 1]),
                            "vW", 1)

            if prev_ln2:
                @blk.scalar
                def _(s_):
                    ln2_scalar(s_, lb)

            @blk.tensor
            def _(t):
                if qsb == 0:
                    t.wait_ge(sem("vAE"), cnt["vAE"])
                    t.wait_ge(sem("semP"), cnt["semP"])
                    if _DEBUG:
                        t.wait_ge(sem("dDBG"), 16)
                else:
                    t.wait_ge(sem("vZ"), vZ_prev)
                    t.wait_ge(sem("vB"), vB_prev)
                for og in range(16):
                    idx = 16 * qsb + og
                    t.wait_ge(sem("dWO%d" % (idx % 2)), (idx // 2 + 1) * 32)
                    t.wait_ge(sem("dXR%d" % (idx % 2)), (idx // 2 + 1) * 64)
                    for qt in range(4):
                        par = qt % 2
                        gq = idx * 4 + qt
                        if gq >= 2:
                            t.wait_ge(sem("vW"), gq - 1)
                        for dt in range(NDT):
                            t.matmul(pS[par][:, 0:256],
                                     a0b[:, 16384 * qsb + 512 * dt + 128 * qt:
                                         16384 * qsb + 512 * dt + 128 * (qt + 1)],
                                     slb[:, 8192 * (idx % 2) + 256 * dt:
                                         8192 * (idx % 2) + 256 * (dt + 1)],
                                     start=(dt == 0), stop=False)
                        inc(t.matmul(pS[par][:, 0:256], ones1b[:],
                                     wobt[0:1, 256 * (idx % 2):256 * (idx % 2) + 256],
                                     start=False, stop=True), "tW", 1)

    def ln2_scalar(s_, lb):
        pq = lb["pq"]
        ph1 = [a1f[:, 4096 * q:4096 * (q + 1)] for q in range(4)]
        phT0 = 16384 * pq
        for qt in range(4):
            s_.wait_ge(sem("vB"), lb["vB2"] + 8 * (qt + 1))
            if qt:
                s_.wait_ge(sem("sLN"), lb["sq2"] + qt)
            inc(s_.activation(
                a0b[:, phT0 + 4096 * qt:phT0 + 4096 * (qt + 1)],
                ph1[qt], AF.Square, bias=0.0, scale=1.0,
                accum_out=st4b[:, qt:qt + 1]), "sLN", 1)
        s_.wait_ge(sem("vLN"), lb["vLN9"])
        inc(s_.activation(st4c[:], st4b[:], AF.Ln, bias=0.0,
                          scale=1.0), "sLN", 1)
        s_.wait_ge(sem("sLN"), lb["sq2"] + 5)
        inc(s_.activation(st4b[:], st4c[:], AF.Exp, bias=0.0,
                          scale=-0.5), "sLN", 1)
        s_.wait_ge(sem("vLN"), lb["vLN11"])
        for qt in range(4):
            inc(s_.activation(ph1[qt], ph1[qt], AF.Identity,
                              bias=st4c[:, qt:qt + 1],
                              scale=st4b[:, qt:qt + 1]), "sOUT", 1)

    def ln2_vector(v, lb):
        for qt in range(4):
            v.wait_ge(sem("vB"), lb["vB2"] + 8 * (qt + 1))
            inc(v.reduce_sum(st4a[:, qt:qt + 1],
                             s1p[:, 16 * qt:16 * qt + 8], AX.X), "vLN", 1)
        v.wait_ge(sem("sLN"), lb["sq2"] + 4)
        v.wait_ge(sem("vLN"), cnt["vLN"])
        inc(v.tensor_scalar_mul(st4a[:], st4a[:], INV_D), "vLN", 1)
        v.wait_ge(sem("vLN"), cnt["vLN"])
        inc(v.tensor_scalar_mul(st4b[:], st4b[:], INV_D), "vLN", 1)
        v.wait_ge(sem("vLN"), cnt["vLN"])
        inc(v.tensor_tensor(st4c[:], st4a[:], st4a[:], ALU.mult), "vLN", 1)
        v.wait_ge(sem("vLN"), cnt["vLN"])
        inc(v.tensor_sub(st4b[:], st4b[:], st4c[:]), "vLN", 1)
        v.wait_ge(sem("vLN"), cnt["vLN"])
        inc(v.tensor_scalar_add(st4b[:], st4b[:], EPS), "vLN", 1)
        lb["vLN9"] = cnt["vLN"]
        v.wait_ge(sem("sLN"), lb["sq2"] + 6)
        inc(v.tensor_tensor(st4c[:], st4a[:], st4b[:], ALU.mult), "vLN", 1)
        v.wait_ge(sem("vLN"), cnt["vLN"])
        inc(v.tensor_scalar_mul(st4c[:], st4c[:], -1.0), "vLN", 1)
        lb["vLN11"] = cnt["vLN"]

    def ln2_store(sy, lb):
        pq = lb["pq"]
        ph1 = [a1f[:, 4096 * q:4096 * (q + 1)] for q in range(4)]
        for qt in range(4):
            sy.wait_ge(sem("sOUT"), lb["sOUT"] + qt + 1)
            inc(sy.dma_start(
                out=d_y[512 * pq + 128 * qt:512 * pq + 128 * (qt + 1), :],
                in_=ph1[qt]), "dY%d" % qt, 16)

    def emit_E_front(qsb):
        h1r = [a1f[:, 4096 * q:4096 * (q + 1)] for q in range(4)]
        hT0 = 16384 * qsb
        with nc.Block() as blk:
            vW_all = cnt["vW"]
            tW_all = cnt["tW"]
            sq_base = cnt["sLN"]
            tT2_base = cnt["tT2"]
            vDC_base = cnt["vDC"]

            @blk.scalar
            def _(s_):
                s_.wait_ge(sem("vW"), vW_all)
                s_.wait_ge(sem("tW"), tW_all)
                for qt in range(2):
                    if qt:
                        s_.wait_ge(sem("sLN"), cnt["sLN"])
                    inc(s_.activation(
                        slb[:, 4096 * (qt % 2):4096 * (qt % 2) + 4096],
                        h1r[qt], AF.Square, bias=0.0, scale=1.0,
                        accum_out=st4b[:, qt:qt + 1]), "sLN", 1)

            @blk.vector
            def _(v):
                v.wait_ge(sem("vW"), vW_all)
                for qt in (2, 3):
                    inc(v.scalar_tensor_tensor(
                        dcbuf[qt % 2], h1r[qt], 1.0, h1r[qt],
                        ALU.mult, ALU.mult,
                        accum_out=st4b[:, qt:qt + 1]), "vLN", 1)
                for qt in range(4):
                    inc(v.reduce_sum(st4a[:, qt:qt + 1],
                                     s1p[:, 16 * qt:16 * qt + 16], AX.X),
                        "vLN", 1)
                v.wait_ge(sem("sLN"), sq_base + 2)
                v.wait_ge(sem("vLN"), cnt["vLN"])
                inc(v.tensor_scalar_mul(st4a[:], st4a[:], INV_D), "vLN", 1)
                v.wait_ge(sem("vLN"), cnt["vLN"])
                inc(v.tensor_scalar_mul(st4b[:], st4b[:], INV_D), "vLN", 1)
                v.wait_ge(sem("vLN"), cnt["vLN"])
                inc(v.tensor_tensor(st4c[:], st4a[:], st4a[:], ALU.mult),
                    "vLN", 1)
                v.wait_ge(sem("vLN"), cnt["vLN"])
                inc(v.tensor_sub(st4b[:], st4b[:], st4c[:]), "vLN", 1)
                v.wait_ge(sem("vLN"), cnt["vLN"])
                inc(v.tensor_scalar_add(st4b[:], st4b[:], EPS), "vLN", 1)

            @blk.scalar
            def _(s_):
                s_.wait_ge(sem("vLN"), cnt["vLN"])
                inc(s_.activation(st4c[:], st4b[:], AF.Ln, bias=0.0,
                                  scale=1.0), "sLN", 1)
                s_.wait_ge(sem("sLN"), cnt["sLN"])
                inc(s_.activation(st4b[:], st4c[:], AF.Exp, bias=0.0,
                                  scale=-0.5), "sLN", 1)

            @blk.vector
            def _(v):
                v.wait_ge(sem("sLN"), cnt["sLN"])
                inc(v.tensor_tensor(st4c[:], st4a[:], st4b[:], ALU.mult),
                    "vLN", 1)
                v.wait_ge(sem("vLN"), cnt["vLN"])
                inc(v.tensor_scalar_mul(st4c[:], st4c[:], -1.0), "vLN", 1)

            @blk.scalar
            def _(s_):
                s_.wait_ge(sem("vLN"), cnt["vLN"])
                for qt in range(4):
                    inc(s_.activation(h1r[qt], h1r[qt], AF.Identity,
                                      bias=st4c[:, qt:qt + 1],
                                      scale=st4b[:, qt:qt + 1]), "sLN", 1)

            @blk.tensor
            def _(t):
                for qt in range(4):
                    t.wait_ge(sem("vDC"), vDC_base + qt + 1)
                    for dtg in range(8):
                        gidx = cnt["tT2"]
                        par = gidx % 2
                        if gidx >= 2:
                            t.wait_ge(sem("vT2"), gidx - 1)
                        for j in range(4):
                            dt = 4 * dtg + j
                            ins = t.transpose(
                                pSb[par][:, 128 * j:128 * (j + 1)],
                                dcbuf[qt % 2][:, 128 * dt:128 * (dt + 1)],
                                idb[:])
                        inc(ins, "tT2", 1)

            @blk.vector
            def _(v):
                for qt in range(4):
                    v.wait_ge(sem("sLN"), sq_base + 5 + qt)
                    if qt < 2:
                        if tT2_base > 0:
                            v.wait_ge(sem("tT2"), tT2_base)
                    else:
                        v.wait_ge(sem("tT2"), tT2_base + 8 * (qt - 1))
                    inc(v.tensor_copy(dcbuf[qt % 2], h1r[qt]), "vDC", 1)
                    for dtg in range(8):
                        gidx = cnt["vT2"]
                        par = gidx % 2
                        v.wait_ge(sem("tT2"), gidx + 1)
                        dst = a0b[:, hT0:hT0 + 16384].rearrange(
                            "p (dt q) -> p dt q", q=512)[
                            :, 4 * dtg:4 * dtg + 4, 128 * qt:128 * (qt + 1)]
                        inc(v.tensor_copy(
                            dst, pSb[par][:, 0:512]
                                .rearrange("p (j q) -> p j q", j=4)),
                            "vT2", 1)

            @blk.gpsimd
            def _(g):
                g.wait_ge(sem("tW"), tW_all)
                inc(g.dma_start(out=slb[:, 8192:12288],
                                in_=din["bT"][0:128, :]), "dBT", 16)
                inc(g.dma_start(out=slb[:, 12288:16384],
                                in_=din["bT"][128:256, :]), "dBT", 16)
                g.wait_ge(sem("sLN"), sq_base + 2)
                inc(g.dma_start(
                    out=slb[:, 0:8192].rearrange("p (t f) -> p t f", t=NDT),
                    in_=din["aT"][:, 0:256]
                        .rearrange("(t p) f -> p t f", p=128)), "dAT", 16)
                esnap["dAT0"] = cnt["dAT"]
                g.wait_ge(sem("tT2"), tT2_base + 24)
                inc(g.dma_start(out=vS8[:, 0:4096],
                                in_=din["bT"][256:384, :]), "dBT", 16)
                g.wait_ge(sem("tT2"), tT2_base + 32)
                inc(g.dma_start(out=p_sb4[:, 0:4096],
                                in_=din["bT"][384:512, :]), "dBT", 16)
                esnap["dBT4"] = cnt["dBT"]

        with nc.Block() as blk:
            @blk.tensor
            def _(t):
                t.wait_ge(sem("vT2"), cnt["vT2"])
                for dt in range(NDT):
                    ins = t.matmul(pbB[0:16, 0:512],
                                   wqr_sb[:, 16 * dt:16 * (dt + 1)],
                                   a0b[:, hT0 + 512 * dt:hT0 + 512 * (dt + 1)],
                                   start=(dt == 0), stop=(dt == NDT - 1))
                inc(ins, "tR", 1)

            @blk.scalar
            def _(s_):
                s_.wait_ge(sem("tR"), cnt["tR"])
                inc(s_.copy(qr_sbT[:], pbB[0:16, 0:512]), "sR", 1)

            @blk.tensor
            def _(t):
                t.wait_ge(sem("sR"), cnt["sR"])
                for qt in range(4):
                    inc(t.matmul(pbA[:, 64 * qt:64 * (qt + 1)],
                                 qr_sbT[:, 128 * qt:128 * (qt + 1)], keysb[:],
                                 start=True, stop=True), "tSc", 1)

            @blk.vector
            def _(v):
                v.wait_ge(sem("tSc"), cnt["tSc"])
                inc(v.tensor_copy(sc_sb[:], pbA[:, 0:256]), "vSc", 1)

        base_tZ = cnt["tZ"]
        with nc.Block() as blk:
            @blk.gpsimd
            def _(g):
                g.wait_ge(sem("tZ"), base_tZ + 2)
                inc(g.dma_start(
                    out=slb[:, 0:8192].rearrange("p (t f) -> p t f", t=NDT),
                    in_=din["aT"][:, 256:512]
                        .rearrange("(t p) f -> p t f", p=128)), "dAT", 16)

            @blk.tensor
            def _(t):
                t.wait_ge(sem("vW"), vW_all)
                t.wait_ge(sem("vT2"), cnt["vT2"])
                zdst = [pS[0], pS[1], pZ[:, 0:512], pZ[:, 512:1024]]
                for ha in range(2):
                    t.wait_ge(sem("dAT"), esnap["dAT0"] + 16 * ha)
                    for mm_ in range(2):
                        m = 2 * ha + mm_
                        for dt in range(NDT):
                            ins = t.matmul(zdst[m],
                                           slb[:, 256 * dt + 128 * mm_:
                                               256 * dt + 128 * (mm_ + 1)],
                                           a0b[:, hT0 + 512 * dt:hT0 + 512 * (dt + 1)],
                                           start=(dt == 0), stop=(dt == NDT - 1))
                        inc(ins, "tZ", 1)

        for qt in range(4):
            with nc.Block() as blk:
                @blk.vector
                def _(v):
                    sc = sc_sb[:, 64 * qt:64 * (qt + 1)]
                    v.wait_ge(sem("vSc"), cnt["vSc"])
                    if qt == 0:
                        v.wait_ge(sem("sTK"), cnt["sTK"])
                    inc(v.max(out=mx_sb[:, 0:8], in_=sc), "vTK", 1)
                    v.wait_ge(sem("vTK"), cnt["vTK"])
                    inc(v.tensor_scalar(fl_sb[:], sc, mx_sb[:, 7:8], None,
                                        ALU.is_ge), "vTK", 1)
                    v.wait_ge(sem("vTK"), cnt["vTK"])
                    inc(v.scalar_tensor_tensor(zz_sb[:], fl_sb[:], -1e30, sc,
                                               ALU.mult, ALU.add), "vTK", 1)
                    v.wait_ge(sem("vTK"), cnt["vTK"])
                    inc(v.max(out=mx_sb[:, 8:16], in_=zz_sb[:]), "vTK", 1)
                    v.wait_ge(sem("vTK"), cnt["vTK"])
                    inc(v.tensor_scalar(msk_sb[:], sc, mx_sb[:, 15:16], None,
                                        ALU.is_ge), "vTK", 1)
                    inc(v.reduce_max(st_d[:], sc, AX.X, negate=True), "vTK", 1)

                @blk.scalar
                def _(s_):
                    s_.wait_ge(sem("vTK"), cnt["vTK"])
                    inc(s_.activation(ex_sb[:], sc_sb[:, 64 * qt:64 * (qt + 1)],
                                      AF.Exp, bias=st_d[:], scale=1.0), "sTK", 1)

                @blk.vector
                def _(v):
                    v.wait_ge(sem("sTK"), cnt["sTK"])
                    inc(v.scalar_tensor_tensor(ge_sb[:], msk_sb[:], 1.0, ex_sb[:],
                                               ALU.mult, ALU.mult,
                                               accum_out=st_d[:]), "vTK", 1)
                    v.wait_ge(sem("vTK"), cnt["vTK"])
                    inc(v.reciprocal(st_c[:], st_d[:]), "vTK", 1)
                    v.wait_ge(sem("vTK"), cnt["vTK"])
                    inc(v.tensor_scalar_mul(gate_sb[:], ge_sb[:], st_c[:]), "vTK", 1)

                @blk.tensor
                def _(t):
                    t.wait_ge(sem("vTK"), cnt["vTK"])
                    inc(t.transpose(pbAb[0:64, 1024:1152], gate_sb[:], idb[:]),
                        "tGT", 1)

                @blk.scalar
                def _(s_):
                    s_.wait_ge(sem("tGT"), cnt["tGT"])
                    inc(s_.copy(gT_sb[:], pbAb[0:64, 1024:1152]), "sGT", 1)

                @blk.tensor
                def _(t):
                    t.wait_ge(sem("sGT"), cnt["sGT"])
                    t.wait_ge(sem("dE8"), 16)
                    for m in range(4):
                        inc(t.matmul(pbB[:, 512 + 128 * m:512 + 128 * (m + 1)],
                                     e8b[:, 128 * m:128 * (m + 1)], gT_sb[:],
                                     start=True, stop=True), "tGX", 1)

                @blk.scalar
                def _(s_):
                    s_.wait_ge(sem("tGX"), cnt["tGX"])
                    dst = gx.rearrange("p (m q) -> p m q", m=4)[
                        :, :, 128 * qt:128 * (qt + 1)]
                    src = pbB[:, 512:1024].rearrange("p (m q) -> p m q", m=4)
                    inc(s_.copy(dst, src), "sGX", 1)

        with nc.Block() as blk:
            @blk.vector
            def _(v):
                zsrc = [pS[0], pS[1], pZ[:, 0:512], pZ[:, 512:1024]]
                v.wait_ge(sem("sGX"), cnt["sGX"])
                for m in range(4):
                    v.wait_ge(sem("tZ"), base_tZ + m + 1)
                    inc(v.tensor_tensor(gx[:, 512 * m:512 * (m + 1)], zsrc[m],
                                        gx[:, 512 * m:512 * (m + 1)], ALU.mult),
                        "vZ", 1)

        for qt in range(4):
            with nc.Block() as blk:
                base_tB = cnt["tB"]
                base_vB = cnt["vB"]

                @blk.tensor
                def _(t):
                    t.wait_ge(sem("dBT"), esnap["dBT4"])
                    t.wait_ge(sem("vZ"), cnt["vZ"])
                    for oc in range(8):
                        par = oc % 2
                        if oc >= 2:
                            t.wait_ge(sem("vB"), base_vB + oc - 1)
                        elif base_vB > 0:
                            t.wait_ge(sem("vB"), base_vB - 1 + oc)
                        rhs = [slb[:, 8192 + 512 * oc:8192 + 512 * (oc + 1)],
                               slb[:, 12288 + 512 * oc:12288 + 512 * (oc + 1)],
                               vS8[:, 512 * oc:512 * (oc + 1)],
                               p_sb4[:, 512 * oc:512 * (oc + 1)]]
                        for m in range(4):
                            ins = t.matmul(
                                pS[par],
                                gx[:, 512 * m + 128 * qt:512 * m + 128 * (qt + 1)],
                                rhs[m],
                                start=(m == 0), stop=(m == 3))
                        inc(ins, "tB", 1)

                @blk.vector
                def _(v):
                    for oc in range(8):
                        par = oc % 2
                        v.wait_ge(sem("tB"), base_tB + oc + 1)
                        inc(v.scalar_tensor_tensor(
                            h1r[qt][:, 512 * oc:512 * (oc + 1)], pS[par], 2.0,
                            h1r[qt][:, 512 * oc:512 * (oc + 1)],
                            ALU.mult, ALU.add,
                            accum_out=s1p[:, 16 * qt + oc:16 * qt + oc + 1]),
                            "vB", 1)


    def emit_E_ln2(qsb):
        h1r = [a1f[:, 4096 * q:4096 * (q + 1)] for q in range(4)]
        hT0 = 16384 * qsb
        with nc.Block() as blk:
            vB2_base = cnt["vB"] - 32
            sq2_base = cnt["sLN"]
            sOUT_base = cnt["sOUT"]

            @blk.scalar
            def _(s_):
                for qt in range(4):
                    s_.wait_ge(sem("vB"), vB2_base + 8 * (qt + 1))
                    if qt:
                        s_.wait_ge(sem("sLN"), cnt["sLN"])
                    inc(s_.activation(
                        a0b[:, hT0 + 4096 * qt:hT0 + 4096 * (qt + 1)],
                        h1r[qt], AF.Square, bias=0.0, scale=1.0,
                        accum_out=st4b[:, qt:qt + 1]), "sLN", 1)

            @blk.vector
            def _(v):
                for qt in range(4):
                    v.wait_ge(sem("vB"), vB2_base + 8 * (qt + 1))
                    inc(v.reduce_sum(st4a[:, qt:qt + 1],
                                     s1p[:, 16 * qt:16 * qt + 8], AX.X),
                        "vLN", 1)
                v.wait_ge(sem("sLN"), sq2_base + 4)
                v.wait_ge(sem("vLN"), cnt["vLN"])
                inc(v.tensor_scalar_mul(st4a[:], st4a[:], INV_D), "vLN", 1)
                v.wait_ge(sem("vLN"), cnt["vLN"])
                inc(v.tensor_scalar_mul(st4b[:], st4b[:], INV_D), "vLN", 1)
                v.wait_ge(sem("vLN"), cnt["vLN"])
                inc(v.tensor_tensor(st4c[:], st4a[:], st4a[:], ALU.mult),
                    "vLN", 1)
                v.wait_ge(sem("vLN"), cnt["vLN"])
                inc(v.tensor_sub(st4b[:], st4b[:], st4c[:]), "vLN", 1)
                v.wait_ge(sem("vLN"), cnt["vLN"])
                inc(v.tensor_scalar_add(st4b[:], st4b[:], EPS), "vLN", 1)

            @blk.scalar
            def _(s_):
                s_.wait_ge(sem("vLN"), cnt["vLN"])
                inc(s_.activation(st4c[:], st4b[:], AF.Ln, bias=0.0,
                                  scale=1.0), "sLN", 1)
                s_.wait_ge(sem("sLN"), cnt["sLN"])
                inc(s_.activation(st4b[:], st4c[:], AF.Exp, bias=0.0,
                                  scale=-0.5), "sLN", 1)

            @blk.vector
            def _(v):
                v.wait_ge(sem("sLN"), cnt["sLN"])
                inc(v.tensor_tensor(st4c[:], st4a[:], st4b[:], ALU.mult),
                    "vLN", 1)
                v.wait_ge(sem("vLN"), cnt["vLN"])
                inc(v.tensor_scalar_mul(st4c[:], st4c[:], -1.0), "vLN", 1)

            @blk.scalar
            def _(s_):
                s_.wait_ge(sem("vLN"), cnt["vLN"])
                for qt in range(4):
                    inc(s_.activation(h1r[qt], h1r[qt], AF.Identity,
                                      bias=st4c[:, qt:qt + 1],
                                      scale=st4b[:, qt:qt + 1]), "sOUT", 1)

            @blk.sync
            def _(sy):
                for qt in range(4):
                    sy.wait_ge(sem("sOUT"), sOUT_base + qt + 1)
                    inc(sy.dma_start(
                        out=d_y[512 * qsb + 128 * qt:512 * qsb + 128 * (qt + 1), :],
                        in_=h1r[qt]), "dY%d" % qt, 16)

    emit_D(0, False)
    emit_E_front(0)
    emit_D(1, True)
    emit_E_front(1)
    emit_E_ln2(1)

    with nc.Block() as blk:
        @blk.sync
        def _(sy):
            for qt in range(4):
                sy.wait_ge(sem("dY%d" % qt), cnt["dY%d" % qt])

    return nc


def _host_inputs(L0_attn_x, L0_attn_wq_w, L0_attn_wq_b, L0_attn_wk_w,
                 L0_attn_wk_b, L0_attn_wv_w, L0_attn_wv_b, L0_attn_wo_w,
                 L0_attn_wo_b, L0_ffn_A_w, L0_ffn_B_w, L0_ffn_Wq_w,
                 L0_ffn_keys):
    x = np.ascontiguousarray(np.asarray(L0_attn_x, dtype=np.float32)[0])
    wqT = np.ascontiguousarray(np.asarray(L0_attn_wq_w, np.float32).T)
    wkT = np.ascontiguousarray(np.asarray(L0_attn_wk_w, np.float32).T)
    wvT = np.ascontiguousarray(np.asarray(L0_attn_wv_w, np.float32).T)
    woT = np.ascontiguousarray(np.asarray(L0_attn_wo_w, np.float32).T)
    aT = np.ascontiguousarray(np.asarray(L0_ffn_A_w, np.float32).T)
    bT = np.ascontiguousarray(np.asarray(L0_ffn_B_w, np.float32).T)
    wqrT = np.ascontiguousarray(np.asarray(L0_ffn_Wq_w, np.float32).T)
    keysT = np.ascontiguousarray(np.asarray(L0_ffn_keys, np.float32).T)
    wqb = np.ascontiguousarray(np.asarray(L0_attn_wq_b, np.float32)
                               .reshape(NH, 128).T)
    wkb = np.ascontiguousarray(np.asarray(L0_attn_wk_b, np.float32)
                               .reshape(128, 1))
    wvb = np.ascontiguousarray(np.asarray(L0_attn_wv_b, np.float32)
                               .reshape(128, 1))
    wob = np.ascontiguousarray(np.asarray(L0_attn_wo_b, np.float32)
                               .reshape(1, D))
    ident = np.eye(128, dtype=np.float32)
    jj = np.arange(896)[None, :]
    pp = np.arange(128)[:, None]
    tri = np.where(jj >= pp + 384, 0.0, -30000.0).astype(np.float32)
    tri = np.ascontiguousarray(tri)
    e8 = np.zeros((64, 512), np.float32)
    for n in range(512):
        e8[n // 8, n] = 1.0

    in_maps = []
    for c in range(8):
        j0, j1 = c, 15 - c
        xown = np.ascontiguousarray(
            np.concatenate([x[512 * j0:512 * (j0 + 1)],
                            x[512 * j1:512 * (j1 + 1)]], axis=0))
        tbl = np.zeros((1, 32), np.int32)
        for g in range(7):
            if g < c:
                tbl[0, g] = 512 * g
                tbl[0, 7 + g] = 0
            else:
                tbl[0, g] = 4096 + 512 * (g - c)
                tbl[0, 7 + g] = 512
        tbl[0, 14] = 512 * j0
        tbl[0, 15] = 512 * j1
        in_maps.append({
            "xfull": x, "xown": xown, "wqT": wqT, "wkT": wkT, "wvT": wvT,
            "woT": woT, "aT": aT, "bT": bT, "wqrT": wqrT, "keysT": keysT,
            "wqb": wqb, "wkb": wkb, "wvb": wvb, "wob": wob, "ident": ident,
            "tri": tri, "e8": e8, "tbl": tbl,
        })
    return in_maps


def kernel(**inputs):
    in_maps = _host_inputs(**inputs)
    if "nc" not in _CACHE:
        _CACHE["nc"] = build_nc()
    nc = _CACHE["nc"]
    res = run_bass_kernel_spmd(nc, in_maps, core_ids=list(range(8)))
    y = np.zeros((S, D), np.float32)
    for c in range(8):
        yc = res.results[c]["y"]
        y[512 * c:512 * (c + 1)] = yc[0:512]
        y[512 * (15 - c):512 * (16 - c)] = yc[512:1024]
    return y.reshape(1, S, D)


# version 15
# speedup vs baseline: 1.0134x; 1.0010x over previous
import math
import numpy as np
import concourse.bass as bass
import concourse.mybir as mybir
from concourse.bass import RegisterHandles
from concourse.expressions_rust import RuntimeValue
from concourse.bass_utils import run_bass_kernel_spmd

F32 = mybir.dt.float32
BF16 = mybir.dt.bfloat16
I32 = mybir.dt.int32
U8 = mybir.dt.uint8
AF = mybir.ActivationFunctionType
ALU = mybir.AluOpType
AX = mybir.AxisListType
PE = mybir.EngineType.PE
POOL = mybir.EngineType.Pool

S = 8192
D = 4096
NH = 32
QL = 1024
NDT = 32
SCALE = 1.0 / math.sqrt(128.0)
EPS = 1e-5
NE, BN, RD = 64, 512, 16
INV_D = 1.0 / 4096.0
NIT = 68
NSB = 18

_CACHE = {}
_DEBUG = False


def build_nc():
    nc = bass.Bass("TRN2", target_bir_lowering=False)
    cnt = {}
    sems = {}
    for name in ("dC", "dCg", "vC", "dX0", "dX1", "tTP", "vXT", "tKV", "sKV", "tTV", "vTV",
                 "dKV0", "dKV1", "dWQ0", "dWQ1", "tQ", "sQ", "dST", "semS", "semP", "semT",
                 "tST", "sLE", "dRC0", "dRC1", "dRB0", "dRB1", "vAE", "vAEr", "vAC", "tAV", "dWO0", "dWO1", "dXR0", "dXR1", "tW", "vW",
                 "vLN", "sLN", "vDC", "tT2", "vT2", "tR", "sR", "tSc", "vSc",
                 "vTK", "sTK", "tGT", "sGT", "tGX", "sGX", "dAT", "tZ", "vZ",
                 "dBT", "tB", "vB", "sOUT", "dY0", "dY1", "dY2", "dY3", "dE8", "dDBG", "vQC", "dWC", "vPS"):
        sems[name] = nc.alloc_semaphore(name)
        cnt[name] = 0

    def sem(name):
        return sems[name]

    def inc(ins, name, n):
        ins.then_inc(sems[name], n)
        cnt[name] += n
        return ins

    din = {}
    for nm, shp in (("xfull", [S, D]), ("xown", [QL, D]), ("wqT", [D, D]),
                    ("wkT", [D, 128]), ("wvT", [D, 128]), ("woT", [D, D]),
                    ("aT", [D, BN]), ("bT", [BN, D]), ("wqrT", [D, RD]),
                    ("keysT", [RD, NE]), ("wqb", [128, NH]), ("wkb", [128, 1]),
                    ("wvb", [128, 1]), ("wob", [1, D]), ("ident", [128, 128]),
                    ("tri", [128, 896]), ("e8", [64, 512])):
        din[nm] = nc.dram_tensor(nm, shp, F32, kind="ExternalInput").ap()
    d_tbl = nc.dram_tensor("tbl", [1, 32], I32, kind="ExternalInput").ap()
    d_y = nc.dram_tensor("y", [QL, D], F32, kind="ExternalOutput").ap()
    d_kT = nc.dram_tensor("kT_scr", [128, S], BF16, kind="Internal").ap()
    d_v = nc.dram_tensor("v_scr", [128, S], BF16, kind="Internal").ap()
    d_rcp = nc.dram_tensor("rcp_scr", [2, 1024], F32, kind="Internal").ap()
    d_wob = nc.dram_tensor("wo_scr", [D, D], BF16, kind="Internal").ap()
    d_wobb = nc.dram_tensor("wob_scr", [1, D], BF16, kind="Internal").ap()
    if _DEBUG:
        d_dbgq = nc.dram_tensor("dbgq", [128, 32768], BF16, kind="ExternalOutput").ap()
        d_dbga = nc.dram_tensor("dbga", [128, 32768], BF16, kind="ExternalOutput").ap()

    A = nc.alloc_sbuf_tensor
    kT8 = A("kT8", [128, 4096], BF16)
    vS8 = A("vS8", [128, 4096], BF16)
    arena0 = A("arena0", [128, 65536], U8)
    arena1 = A("arena1", [128, 65536], U8)
    slab = A("slab", [128, 49152], U8)
    idb = A("idb", [128, 128], BF16)
    onesb = A("onesb", [128, 1], BF16)
    onesf = A("onesf", [128, 1], F32)
    ones1b = A("ones1b", [1, 128], BF16)
    ones1f = A("ones1f", [1, 128], F32)
    wqb_sb = A("wqb_sb", [128, NH], F32)
    wkb_sb = A("wkb_sb", [128, 1], F32)
    wvb_sb = A("wvb_sb", [128, 1], F32)
    wobt = A("wobt", [1, 512], BF16)
    keysb = A("keysb", [RD, NE], BF16)
    wqr_sb = A("wqr_sb", [128, NDT * RD], BF16)
    tbl_sb = A("tbl_sb", [1, 32], I32)
    s1p = A("s1p", [128, 64], F32)
    st_a = A("st_a", [128, 1], F32)
    st_b = A("st_b", [128, 1], F32)
    st_c = A("st_c", [128, 1], F32)
    st_d = A("st_d", [128, 1], F32)
    st4a = A("st4a", [128, 4], F32)
    st4b = A("st4b", [128, 4], F32)
    st4c = A("st4c", [128, 4], F32)
    mx_sb = A("mx_sb", [128, 16], F32)
    gate_sb = A("gate_sb", [128, 64], BF16)
    gT_sb = A("gT_sb", [64, 128], BF16)
    qr_sbT = A("qr_sbT", [RD, 512], BF16)
    qcur = A("qcur", [128, 1024], BF16)
    p_sb4 = A("p_sb4", [128, 4096], BF16)

    a0b = arena0.ap().bitcast(BF16)
    a0f = arena0.ap().bitcast(F32)
    a1b = arena1.ap().bitcast(BF16)
    a1f = arena1.ap().bitcast(F32)
    slb = slab.ap().bitcast(BF16)
    slf = slab.ap().bitcast(F32)

    wkTb = a1b[:, 16384:20480]
    wvTb = a1b[:, 20480:24576]
    ktmp = slb[:, 16384:17408]
    vtt = slb[:, 17408:18432]
    vtmp = slb[:, 18432:19456]
    kstage = slb[:, 16384:19968]
    vstage = slb[:, 19968:23552]
    kdiag = slb[:, 23552:24576]
    trib = slb[:, 0:896]
    zerb = slb[:, 896:1408]
    vdiag = slb[:, 2432:3456]
    rcp_sb = [slf[0:1, 1728:2752], slf[0:1, 2752:3776]]
    rb_sb = [slf[:, 3776:4800], slf[:, 4800:5824]]
    acc_sb = slf[:, 5824:6848]
    acc0_sb = slf[:, 6848:7872]
    ptmp = slb[:, 15744:16256]
    xres = slf[:, 8192:10240]
    gx = kT8.ap()[:, 0:2048]
    e8b = kT8.ap()[0:64, 2048:2560]
    kf = kT8.ap().bitcast(F32)
    sc_sb = kf[:, 1280:1536]
    fl_sb = kf[:, 1536:1600]
    zz_sb = kf[:, 1600:1664]
    msk_sb = kf[:, 1664:1728]
    ex_sb = kf[:, 1728:1792]
    ge_sb = kf[:, 1792:1856]

    pSA = nc.alloc_psum_tensor("pSA", [128, 2048], F32).ap()
    pA0 = nc.alloc_psum_tensor("pA0", [128, 1024], F32).ap()
    pZ = nc.alloc_psum_tensor("pZ", [128, 1024], F32).ap()
    pSum2 = pZ[0:1, 0:1024]
    pbA = pA0
    pbB = pSA[:, 1024:2048]
    pbSum = pSum2
    pS = [pSA[:, 0:512], pSA[:, 512:1024]]
    pSpair = [pSA[:, 0:1024], pSA[:, 1024:2048]]
    pSh = [[pSA[:, 0:512], pSA[:, 512:1024]],
           [pSA[:, 1024:1536], pSA[:, 1536:2048]]]
    pSAb = pSA.bitcast(BF16)
    pSb = [pSAb[:, 0:1024], pSAb[:, 1024:2048]]
    pbBb = pSAb[:, 2048:4096]
    pbAb = pA0.bitcast(BF16)

    with nc.Block() as blk:
        @blk.sync
        def _(sy):
            for dst, src in ((wqb_sb, din["wqb"]), (wkb_sb, din["wkb"]),
                             (wvb_sb, din["wvb"])):
                inc(sy.dma_start(out=dst[:], in_=src), "dC", 16)
            inc(sy.dma_start(out=tbl_sb[:], in_=d_tbl), "dC", 16)

        @blk.gpsimd
        def _(g):
            inc(g.dma_start(out=idb[:], in_=din["ident"]), "dCg", 16)
            inc(g.dma_start(out=keysb[:], in_=din["keysT"]), "dCg", 16)
            inc(g.dma_start(
                out=wqr_sb[:].rearrange("p (t f) -> p t f", t=NDT),
                in_=din["wqrT"].rearrange("(t p) f -> p t f", p=128)), "dCg", 16)
            inc(g.dma_start(
                out=wkTb[:].rearrange("p (t f) -> p t f", t=NDT),
                in_=din["wkT"].rearrange("(t p) f -> p t f", p=128)), "dCg", 16)
            inc(g.dma_start(
                out=wvTb[:].rearrange("p (t f) -> p t f", t=NDT),
                in_=din["wvT"].rearrange("(t p) f -> p t f", p=128)), "dCg", 16)

        @blk.vector
        def _(v):
            v.memset(onesb[:], 1.0)
            v.memset(onesf[:], 1.0)
            v.memset(ones1b[:], 1.0)
            inc(v.memset(ones1f[:], 1.0), "vC", 1)

    def xbuf(sb):
        return slb if (sb % 2 == 0 or sb >= 16) else a0b

    with nc.Block() as blk:
        @blk.gpsimd
        def _(g):
            for sb in range(NSB):
                if sb >= 2:
                    g.wait_ge(sem("tTP"), ((sb - 1) if sb < 16 else sb) * 32)
                src = din["xfull"] if sb < 16 else din["xown"]
                r0 = 512 * sb if sb < 16 else 512 * (sb - 16)
                for rt in range(4):
                    inc(g.dma_start(
                        out=xbuf(sb)[:, 4096 * rt:4096 * (rt + 1)],
                        in_=src[r0 + 128 * rt:r0 + 128 * (rt + 1), :]),
                        "dX%d" % (sb % 2), 16)

        @blk.tensor
        def _(t):
            t.wait_ge(sem("dC"), cnt["dC"])
            t.wait_ge(sem("dCg"), cnt["dCg"])
            t.wait_ge(sem("vC"), cnt["vC"])
            for sb in range(NSB):
                t.wait_ge(sem("dX%d" % (sb % 2)), (sb // 2 + 1) * 64)
                for dt in range(NDT):
                    gidx = sb * 32 + dt
                    par = gidx % 2
                    if gidx >= 2:
                        t.wait_ge(sem("vXT"), gidx - 1)
                    for rt in range(4):
                        ins = t.transpose(
                            pSb[par][:, 128 * rt:128 * (rt + 1)],
                            xbuf(sb)[:, 4096 * rt + 128 * dt:4096 * rt + 128 * (dt + 1)],
                            idb[:])
                    inc(ins, "tTP", 1)
                if sb < 16:
                    t.wait_ge(sem("vXT"), (sb + 1) * 32)
                    if sb > 0:
                        t.wait_ge(sem("sKV"), sb * 2)
                    for dt in range(NDT):
                        t.matmul(pbA[:, 0:512], wkTb[:, 128 * dt:128 * (dt + 1)],
                                 a1b[:, 512 * dt:512 * (dt + 1)],
                                 start=(dt == 0), stop=(dt == NDT - 1))
                    for dt in range(NDT):
                        ins = t.matmul(pbA[:, 512:1024], wvTb[:, 128 * dt:128 * (dt + 1)],
                                       a1b[:, 512 * dt:512 * (dt + 1)],
                                       start=(dt == 0), stop=(dt == NDT - 1))
                    inc(ins, "tKV", 1)
                    t.wait_ge(sem("sKV"), sb * 2 + 2)
                    if sb > 0:
                        t.wait_ge(sem("vTV"), sb)
                    for j in range(4):
                        ins = t.transpose(
                            pbBb[:, 128 * j:128 * (j + 1)],
                            vtt[:, 512 * (sb % 2) + 128 * j:512 * (sb % 2) + 128 * (j + 1)],
                            idb[:])
                    inc(ins, "tTV", 1)

        @blk.scalar
        def _(s_):
            for sb in range(16):
                s_.wait_ge(sem("tKV"), sb + 1)
                if sb >= 2:
                    s_.wait_ge(sem("dKV%d" % (sb % 2)), (sb // 2) * 32)
                kdst = kT8[:, 512 * sb:512 * (sb + 1)] if sb < 8 \
                    else ktmp[:, 512 * (sb % 2):512 * (sb % 2) + 512]
                inc(s_.activation(kdst, pbA[:, 0:512], AF.Identity,
                                  bias=wkb_sb[:], scale=1.0), "sKV", 1)
                inc(s_.activation(vtt[:, 512 * (sb % 2):512 * (sb % 2) + 512],
                                  pbA[:, 512:1024], AF.Identity,
                                  bias=wvb_sb[:], scale=1.0), "sKV", 1)

        @blk.vector
        def _(v):
            for sb in range(NSB):
                for dt in range(NDT):
                    gidx = sb * 32 + dt
                    par = gidx % 2
                    v.wait_ge(sem("tTP"), gidx + 1)
                    if 0 < sb < 16 and dt == 0:
                        v.wait_ge(sem("tKV"), sb)
                    if sb < 16:
                        dst = a1b[:, 512 * dt:512 * (dt + 1)]
                    else:
                        c0 = 1024 * dt + 512 * (sb - 16)
                        dst = a0b[:, c0:c0 + 512]
                    inc(v.tensor_copy(dst, pSb[par][:, 0:512]), "vXT", 1)
                if sb < 16:
                    v.wait_ge(sem("tTV"), sb + 1)
                    vdst2 = vS8[:, 512 * sb:512 * (sb + 1)] if sb < 8 \
                        else vtmp[:, 512 * (sb % 2):512 * (sb % 2) + 512]
                    inc(v.tensor_copy(vdst2, pbBb[:, 0:512]), "vTV", 1)

        @blk.sync
        def _(sy):
            for sb in range(16):
                sy.wait_ge(sem("sKV"), sb * 2 + 1)
                ksrc = kT8[:, 512 * sb:512 * (sb + 1)] if sb < 8 \
                    else ktmp[:, 512 * (sb % 2):512 * (sb % 2) + 512]
                inc(sy.dma_start(out=d_kT[:, 512 * sb:512 * (sb + 1)], in_=ksrc),
                    "dKV%d" % (sb % 2), 16)
                sy.wait_ge(sem("vTV"), sb + 1)
                vsrc = vS8[:, 512 * sb:512 * (sb + 1)] if sb < 8 \
                    else vtmp[:, 512 * (sb % 2):512 * (sb % 2) + 512]
                inc(sy.dma_start(out=d_v[:, 512 * sb:512 * (sb + 1)], in_=vsrc),
                    "dKV%d" % (sb % 2), 16)

    with nc.Block() as blk:
        @blk.gpsimd
        def _(g):
            for hp in range(16):
                if hp >= 2:
                    g.wait_ge(sem("tQ"), (hp - 1) * 4)
                else:
                    g.wait_ge(sem("tTP"), NSB * 32)
                inc(g.dma_start(
                    out=slb[:, 8192 * (hp % 2):8192 * (hp % 2) + 8192]
                        .rearrange("p (t f) -> p t f", t=NDT),
                    in_=din["wqT"][:, 256 * hp:256 * (hp + 1)]
                        .rearrange("(t p) f -> p t f", p=128)),
                    "dWQ%d" % (hp % 2), 16)

        @blk.tensor
        def _(t):
            t.wait_ge(sem("vXT"), NSB * 32)
            for hp in range(16):
                t.wait_ge(sem("dWQ%d" % (hp % 2)), (hp // 2 + 1) * 16)
                for hh in range(2):
                    h = 2 * hp + hh
                    for qh in range(2):
                        m = hp * 4 + hh * 2 + qh
                        if m >= 2:
                            t.wait_ge(sem("sQ"), m - 1)
                        for dt in range(NDT):
                            ins = t.matmul(
                                pbA[:, 512 * qh:512 * (qh + 1)],
                                slb[:, 8192 * (hp % 2) + 256 * dt + 128 * hh:
                                    8192 * (hp % 2) + 256 * dt + 128 * (hh + 1)],
                                a0b[:, 1024 * dt + 512 * qh:1024 * dt + 512 * (qh + 1)],
                                start=(dt == 0), stop=(dt == NDT - 1))
                        inc(ins, "tQ", 1)

        @blk.scalar
        def _(s_):
            for hp in range(16):
                for hh in range(2):
                    h = 2 * hp + hh
                    for qh in range(2):
                        m = hp * 4 + hh * 2 + qh
                        s_.wait_ge(sem("tQ"), m + 1)
                        inc(s_.activation(
                            a1b[:, 1024 * h + 512 * qh:1024 * h + 512 * (qh + 1)],
                            pbA[:, 512 * qh:512 * (qh + 1)], AF.Identity,
                            bias=wqb_sb[:, h:h + 1], scale=1.0), "sQ", 1)

    with nc.Block() as blk:
        @blk.gpsimd
        def _(g):
            g.wait_ge(sem("dKV0"), 256)
            g.wait_ge(sem("dKV1"), 256)
            _, vals = nc.values_load_multi_w_load_instructions(
                tbl_sb[0:1, 0:16], engines=[POOL],
                min_val=0, max_val=S - 512,
                skip_runtime_bounds_check=True)
            for gi in range(7):
                inc(g.dma_start(out=kstage[:, 512 * gi:512 * (gi + 1)],
                                in_=d_kT[:, bass.ds(vals[gi], 512)]), "dST", 16)
                inc(g.dma_start(out=vstage[:, 512 * gi:512 * (gi + 1)],
                                in_=d_v[:, bass.ds(vals[gi], 512)]), "dST", 16)
            for si in range(2):
                inc(g.dma_start(out=kdiag[:, 512 * si:512 * (si + 1)],
                                in_=d_kT[:, bass.ds(vals[14 + si], 512)]), "dST", 16)
            g.wait_ge(sem("tQ"), NH * 2)
            inc(g.dma_start(out=trib, in_=din["tri"]), "dST", 16)
            for si in range(2):
                inc(g.dma_start(out=vdiag[:, 512 * si:512 * (si + 1)],
                                in_=d_v[:, bass.ds(vals[14 + si], 512)]), "dST", 16)

        @blk.vector
        def _(v):
            v.wait_ge(sem("tQ"), NH * 2)
            inc(v.memset(zerb, 0.0), "vC", 1)

    with nc.Block() as blk:
        @blk.gpsimd
        def _(g):
            for og in range(16):
                g.wait_ge(sem("semT"), (NIT // 2) * ((3 * og) // 2))
                inc(g.dma_start(out=d_wob[:, 256 * og:256 * (og + 1)],
                                in_=din["woT"][:, 256 * og:256 * (og + 1)]),
                    "dWC", 16)
            inc(g.dma_start(out=d_wobb[:], in_=din["wob"]), "dWC", 16)

        if _DEBUG:
            @blk.sync
            def _(sy):
                sy.wait_ge(sem("sQ"), NH * 2)
                inc(sy.dma_start(out=d_dbgq, in_=a1b[:, 0:32768]), "dDBG", 16)

        @blk.tensor
        def _(t):
            t.wait_ge(sem("dST"), cnt["dST"])
            t.wait_ge(sem("vC"), cnt["vC"])
            t.wait_ge(sem("sQ"), cnt["sQ"])
            qreg = [nc.alloc_register(PE, f"qoff{g}") for g in range(7)]
            for g in range(7):
                t.reg_load(qreg[g], tbl_sb[0:1, 7 + g:8 + g])
            qw = [RuntimeValue(RegisterHandles((qreg[g],)), 0, 512)
                  for g in range(7)]
            NP = NIT // 2

            def emit_fold(hh):
                t.wait_ge(sem("vAC"), 20 * (hh + 1))
                t.matmul(pSum2[0:1, 512:1024], onesf[:], acc_sb[:, 0:512],
                         start=False, stop=False)
                t.matmul(pSum2[0:1, 0:512], onesf[:], acc0_sb[:, 0:512],
                         start=False, stop=False)
                t.matmul(pSum2[0:1, 0:512], onesf[:], acc0_sb[:, 512:1024],
                         start=False, stop=True)
                inc(t.matmul(pSum2[0:1, 512:1024], onesf[:],
                             acc_sb[:, 512:1024], start=False, stop=True),
                    "tST", 1)

            for h in range(NH):
                bP = h * NP
                pA = pA0
                t.wait_ge(sem("vQC"), h + 1)
                plan = []
                for i in range(NIT):
                    par = i % 2
                    if i < 32:
                        lhsK = kT8[:, 128 * i:128 * (i + 1)]
                        lhsV = vS8[:, 128 * i:128 * (i + 1)]
                        qrhs = qcur[:, 512:1024]
                        aout = pA[:, 512:1024]
                        sout = pSum2[0:1, 512:1024]
                        diag = False
                    elif i < 60:
                        gi, j = (i - 32) // 4, (i - 32) % 4
                        lhsK = kstage[:, 512 * gi + 128 * j:512 * gi + 128 * (j + 1)]
                        lhsV = vstage[:, 512 * gi + 128 * j:512 * gi + 128 * (j + 1)]
                        qrhs = qcur[:, bass.ds(qw[gi], 512)]
                        aout = pA[:, bass.ds(qw[gi], 512)]
                        sout = pSum2[0:1, bass.ds(qw[gi], 512)]
                        diag = False
                    else:
                        si, dl = (i - 60) // 4, (i - 60) % 4
                        lhsK = kdiag[:, 512 * si + 128 * dl:512 * si + 128 * (dl + 1)]
                        lhsV = vdiag[:, 512 * si + 128 * dl:512 * si + 128 * (dl + 1)]
                        qrhs = qcur[:, 512 * si:512 * (si + 1)]
                        aout = pA[:, 512 * si:512 * (si + 1)]
                        sout = pSum2[0:1, 512 * si:512 * (si + 1)]
                        diag = True
                    plan.append((par, diag, dl if diag else 0, lhsK, lhsV,
                                 qrhs, aout, sout))

                def emit_spair(p):
                    sl = pSh[p % 2]
                    for j in (2 * p, 2 * p + 1):
                        parj, diagj, dlj, lhsKj, _, qrhsj, _, _ = plan[j]
                        if diagj:
                            t.matmul(sl[parj], idb[:],
                                     trib[:, 384 - 128 * dlj:896 - 128 * dlj],
                                     start=True, stop=False)
                            ins = t.matmul(sl[parj], lhsKj, qrhsj, start=False,
                                           stop=True)
                        else:
                            ins = t.matmul(sl[parj], lhsKj, qrhsj, start=True,
                                           stop=True)
                    inc(ins, "semS", 1)

                if h > 0:
                    t.wait_ge(sem("semP"), bP - 1)
                emit_spair(0)
                if h > 0:
                    t.wait_ge(sem("semP"), bP)
                emit_spair(1)
                if h > 0:
                    emit_fold(h - 1)
                t.wait_ge(sem("vAEr"), h)
                t.matmul(pA[:, 0:512], idb[:], zerb, start=True, stop=False)
                t.matmul(pA[:, 512:1024], idb[:], zerb, start=True, stop=False)
                for p in range(NP):
                    t.wait_ge(sem("semP"), bP + p + 1)
                    if p == 16:
                        t.wait_ge(sem("sLE"), 2 * h)
                        t.matmul(pSum2[0:1, 0:512], onesb[:], zerb,
                                 start=True, stop=False)
                        t.matmul(pSum2[0:1, 512:1024], onesb[:], zerb,
                                 start=True, stop=False)
                    for i in (2 * p, 2 * p + 1):
                        par, _, _, _, lhsV, _, aout, sout = plan[i]
                        pcol = 512 * (i % 8)
                        ins = t.matmul(aout, lhsV, p_sb4[:, pcol:pcol + 512],
                                       start=False, stop=False)
                    if 16 <= p < 30:
                        t.wait_ge(sem("vPS"), h * 14 + (p - 16) + 1)
                        ins = t.matmul(plan[2 * p][7], onesb[:], ptmp[:],
                                       start=False, stop=False)
                    inc(ins, "semT", 1)
                    if p + 2 < NP:
                        emit_spair(p + 2)
                t.matmul(pA[:, 0:512], idb[:], zerb, start=False, stop=True)
                inc(t.matmul(pA[:, 512:1024], idb[:], zerb, start=False,
                             stop=True), "tAV", 1)
            emit_fold(NH - 1)

        @blk.scalar
        def _(s_):
            NP = NIT // 2

            def emit_lnexp(hh):
                rp = hh % 2
                s_.wait_ge(sem("tST"), hh + 1)
                if hh >= 2:
                    s_.wait_ge(sem("dRC0" if hh % 2 == 0 else "dRC1"),
                               16 * (hh // 2))
                inc(s_.activation(rcp_sb[rp], pSum2[0:1, :], AF.Ln,
                                  bias=0.0, scale=1.0), "sLE", 1)
                s_.wait_ge(sem("sLE"), 2 * hh + 1)
                inc(s_.activation(rcp_sb[rp], rcp_sb[rp], AF.Exp,
                                  bias=0.0, scale=-1.0), "sLE", 1)

            for h in range(NH):
                bS = h * NP
                bT = h * NP
                for p in range(NP):
                    if p == 2 and h > 0:
                        emit_lnexp(h - 1)
                    s_.wait_ge(sem("semS"), bS + p + 1)
                    if bT + p - 3 > 0:
                        s_.wait_ge(sem("semT"), bT + p - 3)
                    if p < 4:
                        if h > 0:
                            s_.wait_ge(sem("vAC"), 20 * (h - 1) + 17 + p)
                    elif p <= 19:
                        s_.wait_ge(sem("vAC"), 20 * h + p - 3)
                    else:
                        s_.wait_ge(sem("vPS"), h * 14 + p - 19)
                    pcol = 1024 * (p % 4)
                    inc(s_.activation(p_sb4[:, pcol:pcol + 1024],
                                      pSpair[p % 2], AF.Exp,
                                      bias=0.0, scale=SCALE), "semP", 1)
            emit_lnexp(NH - 1)

        @blk.sync
        def _(sy):
            for h in range(NH):
                rp = h % 2
                j = h // 2
                sy.wait_ge(sem("sLE"), 2 * h + 2)
                inc(sy.dma_start(out=d_rcp[rp:rp + 1, :], in_=rcp_sb[rp]),
                    "dRC0" if rp == 0 else "dRC1", 16)
                sy.wait_ge(sem("dRC0" if rp == 0 else "dRC1"), 16 * (j + 1))
                if h >= 2:
                    sy.wait_ge(sem("vAE"), 2 * (h - 1))
                inc(sy.dma_start(out=rb_sb[rp],
                                 in_=d_rcp[rp:rp + 1, :].partition_broadcast(128)),
                    "dRB0" if rp == 0 else "dRB1", 16)

        @blk.vector
        def _(v):
            NP = NIT // 2

            def emit_norm(hh):
                rp = hh % 2
                v.wait_ge(sem("vAEr"), hh + 1)
                v.wait_ge(sem("dRB0" if hh % 2 == 0 else "dRB1"),
                          16 * (hh // 2 + 1))
                for si in range(2):
                    sl = a0b[:, 16384 * si + 512 * hh:
                             16384 * si + 512 * (hh + 1)]
                    inc(v.tensor_tensor(sl, sl,
                                        rb_sb[rp][:, 512 * si:512 * (si + 1)],
                                        ALU.mult), "vAE", 1)

            for h in range(NH):
                if h == 0:
                    v.wait_ge(sem("sQ"), NH * 2)
                else:
                    v.wait_ge(sem("semS"), NP * h)
                inc(v.tensor_copy(qcur[:], a1b[:, 1024 * h:1024 * (h + 1)]),
                    "vQC", 1)
                for p in range(NP):
                    if 16 <= p < 30:
                        v.wait_ge(sem("semP"), h * NP + p + 1)
                        v.wait_ge(sem("semT"), h * NP + p)
                        pcol = 1024 * (p % 4)
                        inc(v.tensor_tensor(
                            ptmp[:],
                            p_sb4[:, pcol:pcol + 512],
                            p_sb4[:, pcol + 512:pcol + 1024], ALU.add),
                            "vPS", 1)
                        continue
                    v.wait_ge(sem("semP"), h * NP + p + 1)
                    pcol = 1024 * (p % 4)
                    dst = acc0_sb if p in (30, 31) else acc_sb
                    if p in (0, 30):
                        if p == 0 and h > 0:
                            v.wait_ge(sem("tST"), h)
                        inc(v.tensor_copy(dst, p_sb4[:, pcol:pcol + 1024]),
                            "vAC", 1)
                    else:
                        v.wait_ge(sem("vAC"), cnt["vAC"])
                        inc(v.tensor_tensor(dst, dst,
                                            p_sb4[:, pcol:pcol + 1024], ALU.add),
                            "vAC", 1)
                    if p == 10 and h >= 2:
                        emit_norm(h - 2)
                v.wait_ge(sem("tAV"), h + 1)
                for si in range(2):
                    ins = v.tensor_copy(
                        a0b[:, 16384 * si + 512 * h:16384 * si + 512 * (h + 1)],
                        pA0[:, 512 * si:512 * (si + 1)])
                inc(ins, "vAEr", 1)
            emit_norm(NH - 2)
            emit_norm(NH - 1)

    dcbuf = [vS8[:, 0:4096], p_sb4[:, 0:4096]]
    esnap = {}

    def emit_D(qsb, prev_ln2):
        tZ_prev = cnt["tZ"]
        tB_prev = cnt["tB"]
        vW_prev = cnt["vW"]
        vB_prev = cnt["vB"]
        vZ_prev = cnt["vZ"]
        if prev_ln2:
            lb = {"vB2": cnt["vB"] - 32, "sq2": cnt["sLN"],
                  "sOUT": cnt["sOUT"], "pq": qsb - 1}
        with nc.Block() as blk:
            @blk.gpsimd
            def _(g):
                if qsb == 0:
                    g.wait_ge(sem("vAE"), 2 * NH)
                    g.wait_ge(sem("dRB0"), 16 * (NH // 2))
                    g.wait_ge(sem("dRB1"), 16 * (NH // 2))
                    g.wait_ge(sem("dWC"), 17 * 16)
                    inc(g.dma_start(out=e8b, in_=din["e8"]), "dE8", 16)
                for og in range(16):
                    idx = 16 * qsb + og
                    if idx == 16:
                        g.wait_ge(sem("tZ"), tZ_prev)
                    if idx == 17:
                        g.wait_ge(sem("tB"), tB_prev)
                    if idx >= 2:
                        g.wait_ge(sem("tW"), (idx - 1) * 4)
                    inc(g.dma_start(
                        out=slb[:, 8192 * (idx % 2):8192 * (idx % 2) + 8192]
                            .rearrange("p (t f) -> p t f", t=NDT),
                        in_=d_wob[:, 256 * og:256 * (og + 1)]
                            .rearrange("(t p) f -> p t f", p=128)),
                        "dWO%d" % (idx % 2), 16)
                    inc(g.dma_start(
                        out=wobt[0:1, 256 * (idx % 2):256 * (idx % 2) + 256],
                        in_=d_wobb[0:1, 256 * og:256 * (og + 1)]),
                        "dWO%d" % (idx % 2), 16)

            @blk.sync
            def _(sy):
                def xres_og(og):
                    idx = 16 * qsb + og
                    if idx >= 2:
                        sy.wait_ge(sem("vW"), (idx - 1) * 4)
                    elif idx == 0:
                        sy.wait_ge(sem("vAE"), 2 * NH)
                        sy.wait_ge(sem("dRB0"), 16 * (NH // 2))
                        sy.wait_ge(sem("dRB1"), 16 * (NH // 2))
                    if idx == 16:
                        sy.wait_ge(sem("vW"), vW_prev)
                    for qt in range(4):
                        inc(sy.dma_start(
                            out=xres[:, 1024 * (idx % 2) + 256 * qt:
                                     1024 * (idx % 2) + 256 * (qt + 1)],
                            in_=din["xown"][512 * qsb + 128 * qt:
                                            512 * qsb + 128 * (qt + 1),
                                            256 * og:256 * (og + 1)]),
                            "dXR%d" % (idx % 2), 16)

                if prev_ln2:
                    xres_og(0)
                    xres_og(1)
                    ln2_store(sy, lb)
                    for og in range(2, 16):
                        xres_og(og)
                else:
                    for og in range(16):
                        xres_og(og)

            if _DEBUG and qsb == 0:
                @blk.sync
                def _(sy):
                    sy.wait_ge(sem("vAE"), 2 * NH)
                    inc(sy.dma_start(out=d_dbga, in_=a0b[:, 0:32768]), "dDBG", 16)

            @blk.vector
            def _(v):
                if prev_ln2:
                    ln2_vector(v, lb)
                for og in range(16):
                    idx = 16 * qsb + og
                    for qt in range(4):
                        par = qt % 2
                        gq = idx * 4 + qt
                        if qsb == 1 and og == 0:
                            v.wait_ge(sem("dY%d" % qt), 16)
                        v.wait_ge(sem("tW"), gq + 1)
                        inc(v.scalar_tensor_tensor(
                            a1f[:, 4096 * qt + 256 * og:4096 * qt + 256 * (og + 1)],
                            pS[par][:, 0:256], 1.0,
                            xres[:, 1024 * (idx % 2) + 256 * qt:
                                 1024 * (idx % 2) + 256 * (qt + 1)],
                            ALU.mult, ALU.add,
                            accum_out=s1p[:, 16 * qt + og:16 * qt + og + 1]),
                            "vW", 1)

            if prev_ln2:
                @blk.scalar
                def _(s_):
                    ln2_scalar(s_, lb)

            @blk.tensor
            def _(t):
                if qsb == 0:
                    t.wait_ge(sem("vAE"), cnt["vAE"])
                    t.wait_ge(sem("semP"), cnt["semP"])
                    if _DEBUG:
                        t.wait_ge(sem("dDBG"), 16)
                else:
                    t.wait_ge(sem("vZ"), vZ_prev)
                    t.wait_ge(sem("vB"), vB_prev)
                for og in range(16):
                    idx = 16 * qsb + og
                    t.wait_ge(sem("dWO%d" % (idx % 2)), (idx // 2 + 1) * 32)
                    t.wait_ge(sem("dXR%d" % (idx % 2)), (idx // 2 + 1) * 64)
                    for qt in range(4):
                        par = qt % 2
                        gq = idx * 4 + qt
                        if gq >= 2:
                            t.wait_ge(sem("vW"), gq - 1)
                        for dt in range(NDT):
                            t.matmul(pS[par][:, 0:256],
                                     a0b[:, 16384 * qsb + 512 * dt + 128 * qt:
                                         16384 * qsb + 512 * dt + 128 * (qt + 1)],
                                     slb[:, 8192 * (idx % 2) + 256 * dt:
                                         8192 * (idx % 2) + 256 * (dt + 1)],
                                     start=(dt == 0), stop=False)
                        inc(t.matmul(pS[par][:, 0:256], ones1b[:],
                                     wobt[0:1, 256 * (idx % 2):256 * (idx % 2) + 256],
                                     start=False, stop=True), "tW", 1)

    def ln2_scalar(s_, lb):
        pq = lb["pq"]
        ph1 = [a1f[:, 4096 * q:4096 * (q + 1)] for q in range(4)]
        phT0 = 16384 * pq
        for qt in range(4):
            s_.wait_ge(sem("vB"), lb["vB2"] + 8 * (qt + 1))
            if qt:
                s_.wait_ge(sem("sLN"), lb["sq2"] + qt)
            inc(s_.activation(
                a0b[:, phT0 + 4096 * qt:phT0 + 4096 * (qt + 1)],
                ph1[qt], AF.Square, bias=0.0, scale=1.0,
                accum_out=st4b[:, qt:qt + 1]), "sLN", 1)
        s_.wait_ge(sem("vLN"), lb["vLN9"])
        inc(s_.activation(st4c[:], st4b[:], AF.Ln, bias=0.0,
                          scale=1.0), "sLN", 1)
        s_.wait_ge(sem("sLN"), lb["sq2"] + 5)
        inc(s_.activation(st4b[:], st4c[:], AF.Exp, bias=0.0,
                          scale=-0.5), "sLN", 1)
        s_.wait_ge(sem("vLN"), lb["vLN11"])
        for qt in range(4):
            inc(s_.activation(ph1[qt], ph1[qt], AF.Identity,
                              bias=st4c[:, qt:qt + 1],
                              scale=st4b[:, qt:qt + 1]), "sOUT", 1)

    def ln2_vector(v, lb):
        for qt in range(4):
            v.wait_ge(sem("vB"), lb["vB2"] + 8 * (qt + 1))
            inc(v.reduce_sum(st4a[:, qt:qt + 1],
                             s1p[:, 16 * qt:16 * qt + 8], AX.X), "vLN", 1)
        v.wait_ge(sem("sLN"), lb["sq2"] + 4)
        v.wait_ge(sem("vLN"), cnt["vLN"])
        inc(v.tensor_scalar_mul(st4a[:], st4a[:], INV_D), "vLN", 1)
        v.wait_ge(sem("vLN"), cnt["vLN"])
        inc(v.tensor_scalar_mul(st4b[:], st4b[:], INV_D), "vLN", 1)
        v.wait_ge(sem("vLN"), cnt["vLN"])
        inc(v.tensor_tensor(st4c[:], st4a[:], st4a[:], ALU.mult), "vLN", 1)
        v.wait_ge(sem("vLN"), cnt["vLN"])
        inc(v.tensor_sub(st4b[:], st4b[:], st4c[:]), "vLN", 1)
        v.wait_ge(sem("vLN"), cnt["vLN"])
        inc(v.tensor_scalar_add(st4b[:], st4b[:], EPS), "vLN", 1)
        lb["vLN9"] = cnt["vLN"]
        v.wait_ge(sem("sLN"), lb["sq2"] + 6)
        inc(v.tensor_tensor(st4c[:], st4a[:], st4b[:], ALU.mult), "vLN", 1)
        v.wait_ge(sem("vLN"), cnt["vLN"])
        inc(v.tensor_scalar_mul(st4c[:], st4c[:], -1.0), "vLN", 1)
        lb["vLN11"] = cnt["vLN"]

    def ln2_store(sy, lb):
        pq = lb["pq"]
        ph1 = [a1f[:, 4096 * q:4096 * (q + 1)] for q in range(4)]
        for qt in range(4):
            sy.wait_ge(sem("sOUT"), lb["sOUT"] + qt + 1)
            inc(sy.dma_start(
                out=d_y[512 * pq + 128 * qt:512 * pq + 128 * (qt + 1), :],
                in_=ph1[qt]), "dY%d" % qt, 16)

    def emit_E_front(qsb):
        h1r = [a1f[:, 4096 * q:4096 * (q + 1)] for q in range(4)]
        hT0 = 16384 * qsb
        with nc.Block() as blk:
            vW_all = cnt["vW"]
            tW_all = cnt["tW"]
            sq_base = cnt["sLN"]
            tT2_base = cnt["tT2"]
            vDC_base = cnt["vDC"]

            @blk.scalar
            def _(s_):
                s_.wait_ge(sem("vW"), vW_all)
                s_.wait_ge(sem("tW"), tW_all)
                for qt in range(2):
                    if qt:
                        s_.wait_ge(sem("sLN"), cnt["sLN"])
                    inc(s_.activation(
                        slb[:, 4096 * (qt % 2):4096 * (qt % 2) + 4096],
                        h1r[qt], AF.Square, bias=0.0, scale=1.0,
                        accum_out=st4b[:, qt:qt + 1]), "sLN", 1)

            @blk.vector
            def _(v):
                v.wait_ge(sem("vW"), vW_all)
                for qt in (2, 3):
                    inc(v.scalar_tensor_tensor(
                        dcbuf[qt % 2], h1r[qt], 1.0, h1r[qt],
                        ALU.mult, ALU.mult,
                        accum_out=st4b[:, qt:qt + 1]), "vLN", 1)
                for qt in range(4):
                    inc(v.reduce_sum(st4a[:, qt:qt + 1],
                                     s1p[:, 16 * qt:16 * qt + 16], AX.X),
                        "vLN", 1)
                v.wait_ge(sem("sLN"), sq_base + 2)
                v.wait_ge(sem("vLN"), cnt["vLN"])
                inc(v.tensor_scalar_mul(st4a[:], st4a[:], INV_D), "vLN", 1)
                v.wait_ge(sem("vLN"), cnt["vLN"])
                inc(v.tensor_scalar_mul(st4b[:], st4b[:], INV_D), "vLN", 1)
                v.wait_ge(sem("vLN"), cnt["vLN"])
                inc(v.tensor_tensor(st4c[:], st4a[:], st4a[:], ALU.mult),
                    "vLN", 1)
                v.wait_ge(sem("vLN"), cnt["vLN"])
                inc(v.tensor_sub(st4b[:], st4b[:], st4c[:]), "vLN", 1)
                v.wait_ge(sem("vLN"), cnt["vLN"])
                inc(v.tensor_scalar_add(st4b[:], st4b[:], EPS), "vLN", 1)

            @blk.scalar
            def _(s_):
                s_.wait_ge(sem("vLN"), cnt["vLN"])
                inc(s_.activation(st4c[:], st4b[:], AF.Ln, bias=0.0,
                                  scale=1.0), "sLN", 1)
                s_.wait_ge(sem("sLN"), cnt["sLN"])
                inc(s_.activation(st4b[:], st4c[:], AF.Exp, bias=0.0,
                                  scale=-0.5), "sLN", 1)

            @blk.vector
            def _(v):
                v.wait_ge(sem("sLN"), cnt["sLN"])
                inc(v.tensor_tensor(st4c[:], st4a[:], st4b[:], ALU.mult),
                    "vLN", 1)
                v.wait_ge(sem("vLN"), cnt["vLN"])
                inc(v.tensor_scalar_mul(st4c[:], st4c[:], -1.0), "vLN", 1)
            lnst = cnt["vLN"]

            @blk.tensor
            def _(t):
                for qt in range(4):
                    t.wait_ge(sem("vDC"), vDC_base + qt + 1)
                    for dtg in range(8):
                        gidx = cnt["tT2"]
                        par = gidx % 2
                        if gidx >= 2:
                            t.wait_ge(sem("vT2"), gidx - 1)
                        for j in range(4):
                            dt = 4 * dtg + j
                            ins = t.transpose(
                                pSb[par][:, 128 * j:128 * (j + 1)],
                                dcbuf[qt % 2][:, 128 * dt:128 * (dt + 1)],
                                idb[:])
                        inc(ins, "tT2", 1)

            @blk.vector
            def _(v):
                v.wait_ge(sem("vLN"), lnst)
                for qt in range(4):
                    if qt < 2:
                        if tT2_base > 0:
                            v.wait_ge(sem("tT2"), tT2_base)
                    else:
                        v.wait_ge(sem("tT2"), tT2_base + 8 * (qt - 1))
                    inc(v.tensor_scalar(dcbuf[qt % 2], h1r[qt],
                                        st4b[:, qt:qt + 1],
                                        st4c[:, qt:qt + 1],
                                        ALU.mult, ALU.add), "vDC", 1)
                    for dtg in range(8):
                        gidx = cnt["vT2"]
                        par = gidx % 2
                        v.wait_ge(sem("tT2"), gidx + 1)
                        dst = a0b[:, hT0:hT0 + 16384].rearrange(
                            "p (dt q) -> p dt q", q=512)[
                            :, 4 * dtg:4 * dtg + 4, 128 * qt:128 * (qt + 1)]
                        inc(v.tensor_copy(
                            dst, pSb[par][:, 0:512]
                                .rearrange("p (j q) -> p j q", j=4)),
                            "vT2", 1)

            @blk.gpsimd
            def _(g):
                g.wait_ge(sem("tW"), tW_all)
                inc(g.dma_start(out=slb[:, 8192:12288],
                                in_=din["bT"][0:128, :]), "dBT", 16)
                inc(g.dma_start(out=slb[:, 12288:16384],
                                in_=din["bT"][128:256, :]), "dBT", 16)
                g.wait_ge(sem("sLN"), sq_base + 2)
                inc(g.dma_start(
                    out=slb[:, 0:8192].rearrange("p (t f) -> p t f", t=NDT),
                    in_=din["aT"][:, 0:256]
                        .rearrange("(t p) f -> p t f", p=128)), "dAT", 16)
                esnap["dAT0"] = cnt["dAT"]
                g.wait_ge(sem("tT2"), tT2_base + 24)
                inc(g.dma_start(out=vS8[:, 0:4096],
                                in_=din["bT"][256:384, :]), "dBT", 16)
                g.wait_ge(sem("tT2"), tT2_base + 32)
                inc(g.dma_start(out=p_sb4[:, 0:4096],
                                in_=din["bT"][384:512, :]), "dBT", 16)
                esnap["dBT4"] = cnt["dBT"]

        with nc.Block() as blk:
            @blk.tensor
            def _(t):
                t.wait_ge(sem("vT2"), cnt["vT2"])
                for dt in range(NDT):
                    ins = t.matmul(pbB[0:16, 0:512],
                                   wqr_sb[:, 16 * dt:16 * (dt + 1)],
                                   a0b[:, hT0 + 512 * dt:hT0 + 512 * (dt + 1)],
                                   start=(dt == 0), stop=(dt == NDT - 1))
                inc(ins, "tR", 1)

            @blk.scalar
            def _(s_):
                s_.wait_ge(sem("tR"), cnt["tR"])
                inc(s_.copy(qr_sbT[:], pbB[0:16, 0:512]), "sR", 1)

            @blk.tensor
            def _(t):
                t.wait_ge(sem("sR"), cnt["sR"])
                for qt in range(4):
                    inc(t.matmul(pbA[:, 64 * qt:64 * (qt + 1)],
                                 qr_sbT[:, 128 * qt:128 * (qt + 1)], keysb[:],
                                 start=True, stop=True), "tSc", 1)

            @blk.vector
            def _(v):
                v.wait_ge(sem("tSc"), cnt["tSc"])
                inc(v.tensor_copy(sc_sb[:], pbA[:, 0:256]), "vSc", 1)

        base_tZ = cnt["tZ"]
        with nc.Block() as blk:
            @blk.gpsimd
            def _(g):
                g.wait_ge(sem("tZ"), base_tZ + 2)
                inc(g.dma_start(
                    out=slb[:, 0:8192].rearrange("p (t f) -> p t f", t=NDT),
                    in_=din["aT"][:, 256:512]
                        .rearrange("(t p) f -> p t f", p=128)), "dAT", 16)

            @blk.tensor
            def _(t):
                t.wait_ge(sem("vW"), vW_all)
                t.wait_ge(sem("vT2"), cnt["vT2"])
                zdst = [pS[0], pS[1], pZ[:, 0:512], pZ[:, 512:1024]]
                for ha in range(2):
                    t.wait_ge(sem("dAT"), esnap["dAT0"] + 16 * ha)
                    for mm_ in range(2):
                        m = 2 * ha + mm_
                        for dt in range(NDT):
                            ins = t.matmul(zdst[m],
                                           slb[:, 256 * dt + 128 * mm_:
                                               256 * dt + 128 * (mm_ + 1)],
                                           a0b[:, hT0 + 512 * dt:hT0 + 512 * (dt + 1)],
                                           start=(dt == 0), stop=(dt == NDT - 1))
                        inc(ins, "tZ", 1)

        for qt in range(4):
            with nc.Block() as blk:
                @blk.vector
                def _(v):
                    sc = sc_sb[:, 64 * qt:64 * (qt + 1)]
                    v.wait_ge(sem("vSc"), cnt["vSc"])
                    if qt == 0:
                        v.wait_ge(sem("sTK"), cnt["sTK"])
                    inc(v.max(out=mx_sb[:, 0:8], in_=sc), "vTK", 1)
                    v.wait_ge(sem("vTK"), cnt["vTK"])
                    inc(v.tensor_scalar(fl_sb[:], sc, mx_sb[:, 7:8], None,
                                        ALU.is_ge), "vTK", 1)
                    v.wait_ge(sem("vTK"), cnt["vTK"])
                    inc(v.scalar_tensor_tensor(zz_sb[:], fl_sb[:], -1e30, sc,
                                               ALU.mult, ALU.add), "vTK", 1)
                    v.wait_ge(sem("vTK"), cnt["vTK"])
                    inc(v.max(out=mx_sb[:, 8:16], in_=zz_sb[:]), "vTK", 1)
                    v.wait_ge(sem("vTK"), cnt["vTK"])
                    inc(v.tensor_scalar(msk_sb[:], sc, mx_sb[:, 15:16], None,
                                        ALU.is_ge), "vTK", 1)
                    inc(v.reduce_max(st_d[:], sc, AX.X, negate=True), "vTK", 1)

                @blk.scalar
                def _(s_):
                    s_.wait_ge(sem("vTK"), cnt["vTK"])
                    inc(s_.activation(ex_sb[:], sc_sb[:, 64 * qt:64 * (qt + 1)],
                                      AF.Exp, bias=st_d[:], scale=1.0), "sTK", 1)

                @blk.vector
                def _(v):
                    v.wait_ge(sem("sTK"), cnt["sTK"])
                    inc(v.scalar_tensor_tensor(ge_sb[:], msk_sb[:], 1.0, ex_sb[:],
                                               ALU.mult, ALU.mult,
                                               accum_out=st_d[:]), "vTK", 1)
                    v.wait_ge(sem("vTK"), cnt["vTK"])
                    inc(v.reciprocal(st_c[:], st_d[:]), "vTK", 1)
                    v.wait_ge(sem("vTK"), cnt["vTK"])
                    inc(v.tensor_scalar_mul(gate_sb[:], ge_sb[:], st_c[:]), "vTK", 1)

                @blk.tensor
                def _(t):
                    t.wait_ge(sem("vTK"), cnt["vTK"])
                    inc(t.transpose(pbAb[0:64, 1024:1152], gate_sb[:], idb[:]),
                        "tGT", 1)

                @blk.scalar
                def _(s_):
                    s_.wait_ge(sem("tGT"), cnt["tGT"])
                    inc(s_.copy(gT_sb[:], pbAb[0:64, 1024:1152]), "sGT", 1)

                @blk.tensor
                def _(t):
                    t.wait_ge(sem("sGT"), cnt["sGT"])
                    t.wait_ge(sem("dE8"), 16)
                    for m in range(4):
                        inc(t.matmul(pbB[:, 512 + 128 * m:512 + 128 * (m + 1)],
                                     e8b[:, 128 * m:128 * (m + 1)], gT_sb[:],
                                     start=True, stop=True), "tGX", 1)

                @blk.scalar
                def _(s_):
                    s_.wait_ge(sem("tGX"), cnt["tGX"])
                    dst = gx.rearrange("p (m q) -> p m q", m=4)[
                        :, :, 128 * qt:128 * (qt + 1)]
                    src = pbB[:, 512:1024].rearrange("p (m q) -> p m q", m=4)
                    inc(s_.copy(dst, src), "sGX", 1)

        with nc.Block() as blk:
            @blk.vector
            def _(v):
                zsrc = [pS[0], pS[1], pZ[:, 0:512], pZ[:, 512:1024]]
                v.wait_ge(sem("sGX"), cnt["sGX"])
                for m in range(4):
                    v.wait_ge(sem("tZ"), base_tZ + m + 1)
                    inc(v.tensor_tensor(gx[:, 512 * m:512 * (m + 1)], zsrc[m],
                                        gx[:, 512 * m:512 * (m + 1)], ALU.mult),
                        "vZ", 1)

            @blk.scalar
            def _(s_):
                for qt in range(4):
                    s_.wait_ge(sem("vDC"), vDC_base + qt + 1)
                    if qt == 0:
                        s_.wait_ge(sem("vLN"), lnst)
                    inc(s_.activation(h1r[qt], h1r[qt], AF.Identity,
                                      bias=st4c[:, qt:qt + 1],
                                      scale=st4b[:, qt:qt + 1]), "sLN", 1)
        esnap["lnapl"] = cnt["sLN"]

        for qt in range(4):
            with nc.Block() as blk:
                base_tB = cnt["tB"]
                base_vB = cnt["vB"]

                @blk.tensor
                def _(t):
                    t.wait_ge(sem("dBT"), esnap["dBT4"])
                    t.wait_ge(sem("vZ"), cnt["vZ"])
                    for oc in range(8):
                        par = oc % 2
                        if oc >= 2:
                            t.wait_ge(sem("vB"), base_vB + oc - 1)
                        elif base_vB > 0:
                            t.wait_ge(sem("vB"), base_vB - 1 + oc)
                        rhs = [slb[:, 8192 + 512 * oc:8192 + 512 * (oc + 1)],
                               slb[:, 12288 + 512 * oc:12288 + 512 * (oc + 1)],
                               vS8[:, 512 * oc:512 * (oc + 1)],
                               p_sb4[:, 512 * oc:512 * (oc + 1)]]
                        for m in range(4):
                            ins = t.matmul(
                                pS[par],
                                gx[:, 512 * m + 128 * qt:512 * m + 128 * (qt + 1)],
                                rhs[m],
                                start=(m == 0), stop=(m == 3))
                        inc(ins, "tB", 1)

                @blk.vector
                def _(v):
                    v.wait_ge(sem("sLN"), esnap["lnapl"])
                    for oc in range(8):
                        par = oc % 2
                        v.wait_ge(sem("tB"), base_tB + oc + 1)
                        inc(v.scalar_tensor_tensor(
                            h1r[qt][:, 512 * oc:512 * (oc + 1)], pS[par], 2.0,
                            h1r[qt][:, 512 * oc:512 * (oc + 1)],
                            ALU.mult, ALU.add,
                            accum_out=s1p[:, 16 * qt + oc:16 * qt + oc + 1]),
                            "vB", 1)


    def emit_E_ln2(qsb):
        h1r = [a1f[:, 4096 * q:4096 * (q + 1)] for q in range(4)]
        hT0 = 16384 * qsb
        with nc.Block() as blk:
            vB2_base = cnt["vB"] - 32
            sq2_base = cnt["sLN"]
            sOUT_base = cnt["sOUT"]

            @blk.scalar
            def _(s_):
                for qt in range(4):
                    s_.wait_ge(sem("vB"), vB2_base + 8 * (qt + 1))
                    if qt:
                        s_.wait_ge(sem("sLN"), cnt["sLN"])
                    inc(s_.activation(
                        a0b[:, hT0 + 4096 * qt:hT0 + 4096 * (qt + 1)],
                        h1r[qt], AF.Square, bias=0.0, scale=1.0,
                        accum_out=st4b[:, qt:qt + 1]), "sLN", 1)

            @blk.vector
            def _(v):
                for qt in range(4):
                    v.wait_ge(sem("vB"), vB2_base + 8 * (qt + 1))
                    inc(v.reduce_sum(st4a[:, qt:qt + 1],
                                     s1p[:, 16 * qt:16 * qt + 8], AX.X),
                        "vLN", 1)
                v.wait_ge(sem("sLN"), sq2_base + 4)
                v.wait_ge(sem("vLN"), cnt["vLN"])
                inc(v.tensor_scalar_mul(st4a[:], st4a[:], INV_D), "vLN", 1)
                v.wait_ge(sem("vLN"), cnt["vLN"])
                inc(v.tensor_scalar_mul(st4b[:], st4b[:], INV_D), "vLN", 1)
                v.wait_ge(sem("vLN"), cnt["vLN"])
                inc(v.tensor_tensor(st4c[:], st4a[:], st4a[:], ALU.mult),
                    "vLN", 1)
                v.wait_ge(sem("vLN"), cnt["vLN"])
                inc(v.tensor_sub(st4b[:], st4b[:], st4c[:]), "vLN", 1)
                v.wait_ge(sem("vLN"), cnt["vLN"])
                inc(v.tensor_scalar_add(st4b[:], st4b[:], EPS), "vLN", 1)

            @blk.scalar
            def _(s_):
                s_.wait_ge(sem("vLN"), cnt["vLN"])
                inc(s_.activation(st4c[:], st4b[:], AF.Ln, bias=0.0,
                                  scale=1.0), "sLN", 1)
                s_.wait_ge(sem("sLN"), cnt["sLN"])
                inc(s_.activation(st4b[:], st4c[:], AF.Exp, bias=0.0,
                                  scale=-0.5), "sLN", 1)

            @blk.vector
            def _(v):
                v.wait_ge(sem("sLN"), cnt["sLN"])
                inc(v.tensor_tensor(st4c[:], st4a[:], st4b[:], ALU.mult),
                    "vLN", 1)
                v.wait_ge(sem("vLN"), cnt["vLN"])
                inc(v.tensor_scalar_mul(st4c[:], st4c[:], -1.0), "vLN", 1)

            @blk.scalar
            def _(s_):
                s_.wait_ge(sem("vLN"), cnt["vLN"])
                for qt in range(4):
                    inc(s_.activation(h1r[qt], h1r[qt], AF.Identity,
                                      bias=st4c[:, qt:qt + 1],
                                      scale=st4b[:, qt:qt + 1]), "sOUT", 1)

            @blk.sync
            def _(sy):
                for qt in range(4):
                    sy.wait_ge(sem("sOUT"), sOUT_base + qt + 1)
                    inc(sy.dma_start(
                        out=d_y[512 * qsb + 128 * qt:512 * qsb + 128 * (qt + 1), :],
                        in_=h1r[qt]), "dY%d" % qt, 16)

    emit_D(0, False)
    emit_E_front(0)
    emit_D(1, True)
    emit_E_front(1)
    emit_E_ln2(1)

    with nc.Block() as blk:
        @blk.sync
        def _(sy):
            for qt in range(4):
                sy.wait_ge(sem("dY%d" % qt), cnt["dY%d" % qt])

    return nc


def _host_inputs(L0_attn_x, L0_attn_wq_w, L0_attn_wq_b, L0_attn_wk_w,
                 L0_attn_wk_b, L0_attn_wv_w, L0_attn_wv_b, L0_attn_wo_w,
                 L0_attn_wo_b, L0_ffn_A_w, L0_ffn_B_w, L0_ffn_Wq_w,
                 L0_ffn_keys):
    x = np.ascontiguousarray(np.asarray(L0_attn_x, dtype=np.float32)[0])
    wqT = np.ascontiguousarray(np.asarray(L0_attn_wq_w, np.float32).T)
    wkT = np.ascontiguousarray(np.asarray(L0_attn_wk_w, np.float32).T)
    wvT = np.ascontiguousarray(np.asarray(L0_attn_wv_w, np.float32).T)
    woT = np.ascontiguousarray(np.asarray(L0_attn_wo_w, np.float32).T)
    aT = np.ascontiguousarray(np.asarray(L0_ffn_A_w, np.float32).T)
    bT = np.ascontiguousarray(np.asarray(L0_ffn_B_w, np.float32).T)
    wqrT = np.ascontiguousarray(np.asarray(L0_ffn_Wq_w, np.float32).T)
    keysT = np.ascontiguousarray(np.asarray(L0_ffn_keys, np.float32).T)
    wqb = np.ascontiguousarray(np.asarray(L0_attn_wq_b, np.float32)
                               .reshape(NH, 128).T)
    wkb = np.ascontiguousarray(np.asarray(L0_attn_wk_b, np.float32)
                               .reshape(128, 1))
    wvb = np.ascontiguousarray(np.asarray(L0_attn_wv_b, np.float32)
                               .reshape(128, 1))
    wob = np.ascontiguousarray(np.asarray(L0_attn_wo_b, np.float32)
                               .reshape(1, D))
    ident = np.eye(128, dtype=np.float32)
    jj = np.arange(896)[None, :]
    pp = np.arange(128)[:, None]
    tri = np.where(jj >= pp + 384, 0.0, -30000.0).astype(np.float32)
    tri = np.ascontiguousarray(tri)
    e8 = np.zeros((64, 512), np.float32)
    for n in range(512):
        e8[n // 8, n] = 1.0

    in_maps = []
    for c in range(8):
        j0, j1 = c, 15 - c
        xown = np.ascontiguousarray(
            np.concatenate([x[512 * j0:512 * (j0 + 1)],
                            x[512 * j1:512 * (j1 + 1)]], axis=0))
        tbl = np.zeros((1, 32), np.int32)
        for g in range(7):
            if g < c:
                tbl[0, g] = 512 * g
                tbl[0, 7 + g] = 0
            else:
                tbl[0, g] = 4096 + 512 * (g - c)
                tbl[0, 7 + g] = 512
        tbl[0, 14] = 512 * j0
        tbl[0, 15] = 512 * j1
        in_maps.append({
            "xfull": x, "xown": xown, "wqT": wqT, "wkT": wkT, "wvT": wvT,
            "woT": woT, "aT": aT, "bT": bT, "wqrT": wqrT, "keysT": keysT,
            "wqb": wqb, "wkb": wkb, "wvb": wvb, "wob": wob, "ident": ident,
            "tri": tri, "e8": e8, "tbl": tbl,
        })
    return in_maps


def kernel(**inputs):
    in_maps = _host_inputs(**inputs)
    if "nc" not in _CACHE:
        _CACHE["nc"] = build_nc()
    nc = _CACHE["nc"]
    res = run_bass_kernel_spmd(nc, in_maps, core_ids=list(range(8)))
    y = np.zeros((S, D), np.float32)
    for c in range(8):
        yc = res.results[c]["y"]
        y[512 * c:512 * (c + 1)] = yc[0:512]
        y[512 * (15 - c):512 * (16 - c)] = yc[512:1024]
    return y.reshape(1, S, D)
